# version 25
# baseline (speedup 1.0000x reference)
import numpy as np
from contextlib import ExitStack
import concourse.bass as bass
import concourse.mybir as mybir
from concourse.bass_utils import run_bass_kernel_spmd

F32 = mybir.dt.float32
BF16 = mybir.dt.bfloat16
FP8 = mybir.dt.float8e4
AF = mybir.ActivationFunctionType
ALU = mybir.AluOpType
AX = mybir.AxisListType

MAGIC = 12582912.0
NEG = -3.0e38
E, SQ, H, D, INNER = 1024, 2048, 16, 64, 4096
NT = 32
STAGE = 3
DEBUG = 0
SERIAL_F1 = 0
SERIAL_F2 = 0
CKEYS = [0, 1, 2, 3] + list(range(64, 2048, 64))
NCK = 40


def host_constants():
    ident = np.eye(128, dtype=np.float32)
    p = np.arange(128)[:, None]
    q = np.arange(128)[None, :]
    bandA = ((p >= q) | (p % 64 == 0)).astype(np.float32)
    bandA1 = (((p >= q) | (p % 64 == 0)) | (p < 4)).astype(np.float32)
    bandB = ((p <= q) | (p % 64 == 0)).astype(np.float32)
    bandB0 = (((p <= q) | (p % 64 == 0)) | (p < 4)).astype(np.float32)
    band01 = np.concatenate([bandA, bandA1, bandB, bandB0], axis=1)
    ded = np.zeros((NCK, 16 * 128), np.float32)
    for qt in range(16):
        i0 = qt * 128
        for r, j in enumerate(CKEYS):
            if not (i0 - 128 <= j <= i0 + 127):
                ded[r, qt * 128:(qt + 1) * 128] = 1.0
    ones = np.ones((1, 128), np.float32)
    return ident, band01, ded, ones


class B:
    def __init__(self):
        self.nc = bass.Bass("TRN2", target_bir_lowering=False, debug=False)
        self.top = ExitStack()
        self.sem = {}
        self.cnt = {}

    def S(self, name):
        if name not in self.sem:
            self.sem[name] = self.top.enter_context(self.nc.semaphore(name))
            self.cnt[name] = 0
        return self.sem[name]

    def inc(self, inst, name, k=1):
        inst.then_inc(self.S(name), k)
        self.cnt[name] += k
        return self.cnt[name]

    def n(self, name):
        self.S(name)
        return self.cnt[name]

    def chain_inc(self, eng, inst, tag):
        sem = self.S(tag)
        inst.then_inc(sem, 1)
        self.cnt[tag] += 1
        eng.wait_ge(sem, self.cnt[tag])
        return self.cnt[tag]

    def chain(self, eng, inst, tag):
        sem = self.S(tag)
        inst.then_inc(sem, 1)
        self.cnt[tag] += 1
        eng.wait_ge(sem, self.cnt[tag])
        return inst


def build_program():
    b = B()
    nc = b.nc

    d_x = nc.dram_tensor("d_x", [2 * SQ, E], F32, kind="ExternalInput").ap()
    d_wqkv = nc.dram_tensor("d_wqkv", [E, 3 * E], F32, kind="ExternalInput").ap()
    d_bqkv = nc.dram_tensor("d_bqkv", [3 * E], F32, kind="ExternalInput").ap()
    d_wlif = nc.dram_tensor("d_wlif", [H, D], F32, kind="ExternalInput").ap()
    d_blif = nc.dram_tensor("d_blif", [H], F32, kind="ExternalInput").ap()
    d_wout = nc.dram_tensor("d_wout", [E, E], F32, kind="ExternalInput").ap()
    d_bout = nc.dram_tensor("d_bout", [E], F32, kind="ExternalInput").ap()
    d_wg = nc.dram_tensor("d_wg", [INNER, E], F32, kind="ExternalInput").ap()
    d_wu = nc.dram_tensor("d_wu", [INNER, E], F32, kind="ExternalInput").ap()
    d_wd = nc.dram_tensor("d_wd", [E, INNER], F32, kind="ExternalInput").ap()
    d_ident = nc.dram_tensor("d_ident", [128, 128], F32, kind="ExternalInput").ap()
    d_band = nc.dram_tensor("d_band", [128, 512], F32, kind="ExternalInput").ap()
    d_ded = nc.dram_tensor("d_ded", [NCK, 16 * 128], F32, kind="ExternalInput").ap()
    d_ones = nc.dram_tensor("d_ones", [1, 128], F32, kind="ExternalInput").ap()
    d_out = nc.dram_tensor("d_out", [2 * SQ, E], F32, kind="ExternalOutput").ap()
    d_dbg = (nc.dram_tensor("d_dbg", [H, SQ], F32, kind="ExternalOutput").ap()
             if STAGE < 3 else None)
    d_y2t = nc.dram_tensor("d_y2t", [2, E, SQ], BF16, kind="Internal").ap()
    d_tern = nc.dram_tensor("d_tern", [3, 128, 8 * INNER], FP8, kind="Internal").ap()
    d_ccin = nc.dram_tensor("d_ccin", [H, SQ], F32, kind="Internal").ap()
    d_ccout = nc.dram_tensor("d_ccout", [H, SQ], F32, kind="Internal").ap()
    d_misc = nc.dram_tensor("d_misc", [16, 128], F32, kind="Internal").ap()
    if DEBUG:
        d_dwc = nc.dram_tensor("d_dwc", [128, 8], F32, kind="ExternalOutput").ap()
        d_dtg = nc.dram_tensor("d_dtg", [128, 4096], FP8, kind="ExternalOutput").ap()
        d_dy1 = nc.dram_tensor("d_dy1", [128, E], F32, kind="ExternalOutput").ap()
        d_dxq = nc.dram_tensor("d_dxq", [128, E], BF16, kind="ExternalOutput").ap()
        d_dsg = nc.dram_tensor("d_dsg", [128, 512], F32, kind="ExternalOutput").ap()
        d_dh = nc.dram_tensor("d_dh", [128, INNER], F32, kind="ExternalOutput").ap()
        d_dsc = nc.dram_tensor("d_dsc", [64, 128], BF16, kind="ExternalOutput").ap()
        d_dxqt = nc.dram_tensor("d_dxqt", [128, 1024], BF16, kind="ExternalOutput").ap()
        d_dmr = nc.dram_tensor("d_dmr", [1, 512], F32, kind="ExternalOutput").ap()
        d_dacc = nc.dram_tensor("d_dacc", [128, 36], F32, kind="ExternalOutput").ap()
        d_dtg2 = nc.dram_tensor("d_dtg2", [128, 4096], FP8, kind="ExternalOutput").ap()
        d_dtu = nc.dram_tensor("d_dtu", [128, 8 * 4096], FP8, kind="ExternalOutput").ap()
        d_dtd = nc.dram_tensor("d_dtd", [128, 32 * 1024], FP8, kind="ExternalOutput").ap()
        d_dst35 = nc.dram_tensor("d_dst35", [128, 1024], F32, kind="ExternalOutput").ap()
        d_dtq35 = nc.dram_tensor("d_dtq35", [128, 1024], BF16, kind="ExternalOutput").ap()
        d_dsnap = nc.dram_tensor("d_dsnap", [64, 128, 8], F32, kind="ExternalOutput").ap()

    t_ = b.top.enter_context
    resid = t_(nc.sbuf_tensor("resid", [128, NT * E], BF16))
    identb = t_(nc.sbuf_tensor("identb", [128, 128], BF16))
    ones1b = t_(nc.sbuf_tensor("ones1b", [1, 128], BF16))
    wqlif = t_(nc.sbuf_tensor("wqlif", [128, 8 * 2 * H], BF16))
    blife = t_(nc.sbuf_tensor("blife", [64, 1], F32))
    bqkvc = t_(nc.sbuf_tensor("bqkvc", [128, 32], F32))
    wcols = t_(nc.sbuf_tensor("wcols", [128, 8], F32))
    counts = t_(nc.sbuf_tensor("counts", [H, SQ], BF16))
    scanIc = t_(nc.sbuf_tensor("scanIc", [64, 256], F32))
    cmat = t_(nc.sbuf_tensor("cmat", [64, 128], F32))
    vcol = t_(nc.sbuf_tensor("vcol", [64, 128], F32))
    mwid = t_(nc.sbuf_tensor("mwid", [64, 128], F32))
    colsA = t_(nc.sbuf_tensor("colsA", [128, 32], F32))
    itmp = t_(nc.sbuf_tensor("itmp", [128, 32], F32))
    escr = t_(nc.sbuf_tensor("escr", [64, 128], BF16))
    escr2 = t_(nc.sbuf_tensor("escr2", [16, 128], BF16))

    def emit_scan_chunk(jc):
        if DEBUG and jc == 0:
            @blk.vector
            def _(v):
                b.inc(v.tensor_copy(escr[:], scanIc[0:64, 0:128]), "dsc")

            wdsc = b.n("dsc")

            @blk.sync
            def _(sync):
                sync.wait_ge(b.S("dsc"), wdsc)
                b.inc(sync.dma_start(out=d_dsc, in_=escr[:]), "dbg", 16)
        for part in range(4):
            emit_scan_part(jc, part)

    def emit_scan_part(jc, part):
        @blk.vector
        def _(v, jc=jc, part=part):
            base = (jc % 2) * 128
            ins = None
            for i in range(part * 32, part * 32 + 32):
                icol = scanIc[:, base + i:base + i + 1]
                if jc == 0 and i == 0:
                    v.memset(vcol[:], 0.0)
                    v.tensor_scalar(vcol[:], vcol[:], 1.0, icol,
                                    op0=ALU.mult, op1=ALU.add)
                else:
                    v.tensor_scalar(vcol[:], vcol[:], mwid[:, 0:1], icol,
                                    op0=ALU.mult, op1=ALU.add)
                v.tensor_copy(cmat[:, i:i + 1], vcol[:, 0:1])
                ins = v.tensor_scalar(mwid[:], vcol[:], 1.0, 0.9,
                                      op0=ALU.is_lt, op1=ALU.mult)
            if part == 3:
                b.chain(v, ins, "vch")
                b.chain(v, v.tensor_scalar(escr[:], cmat[:], 1.0, 1.0,
                                           op0=ALU.is_lt, op1=ALU.is_lt), "vch")
                b.chain(v, v.tensor_copy(escr2[:], escr[32:48, :]), "vch")
                v.tensor_add(counts[:, jc * 128:(jc + 1) * 128],
                             escr[0:16, :], escr2[:])
                b.inc(v.memset(cmat[:, 0:1], 0.0), "vcnt")

    with nc.Block() as blk:
        inita = ExitStack()
        wqq = inita.enter_context(nc.sbuf_tensor("wqq", [128, 8 * E], F32, side="right"))
        idf = inita.enter_context(nc.sbuf_tensor("idf", [128, 128], F32, side="right"))
        hdt = inita.enter_context(nc.sbuf_tensor("hdt", [16, 64], F32, side="right"))
        wlf = inita.enter_context(nc.sbuf_tensor("wlf", [16, 64], F32, side="right"))
        wlbc = inita.enter_context(nc.sbuf_tensor("wlbc", [128, E], F32, side="right"))
        wlfr = inita.enter_context(nc.sbuf_tensor("wlfr", [1, E], F32, side="right"))
        wlfh = inita.enter_context(nc.sbuf_tensor("wlfh", [1, E], BF16, side="right"))
        wlfl = inita.enter_context(nc.sbuf_tensor("wlfl", [1, E], BF16, side="right"))
        wlft = inita.enter_context(nc.sbuf_tensor("wlft", [1, E], F32, side="right"))
        wqlf = inita.enter_context(nc.sbuf_tensor("wqlf", [128, 8 * H], F32, side="right"))
        bqn = inita.enter_context(nc.sbuf_tensor("bqn", [32, 128], F32, side="right"))
        rows = inita.enter_context(nc.sbuf_tensor("rows", [1, 128], F32, side="right"))
        tsc = inita.enter_context(nc.sbuf_tensor("tsc", [32, 32], F32, side="right"))
        tsc2 = inita.enter_context(nc.sbuf_tensor("tsc2", [32, 32], F32, side="right"))
        prodw = inita.enter_context(nc.sbuf_tensor("prodw", [128, E], F32, side="right"))
        ipb = inita.enter_context(nc.psum_tensor("ipb", [128, 512], F32, side="right"))

        @blk.vector
        def _(v):
            v.memset(tsc[:], 0.0)
            v.memset(tsc2[:], 0.0)
            v.memset(bqn[:], 0.0)
            v.memset(blife[:], 0.0)
            v.memset(vcol[:], 0.0)
            v.memset(itmp[:], 0.0)
            v.memset(cmat[:], 0.0)
            v.memset(counts[:], 0.0)
            v.memset(wcols[:], 0.0)
            b.inc(v.memset(scanIc[:], 0.0), "vz")

        wz = b.n("vz")

        @blk.sync
        def _(sync):
            sync.wait_ge(b.S("vz"), wz)
            b.inc(sync.dma_start(out=idf[:], in_=d_ident), "ld", 16)
            b.inc(sync.dma_start(out=hdt[:], in_=d_bqkv[0:E].rearrange("(h d) -> h d", d=64)),
                  "ld", 16)
            b.inc(sync.dma_start(out=wlf[:], in_=d_wlif), "ld", 16)
            b.inc(sync.dma_start(out=tsc[0:1, 0:16], in_=d_blif.rearrange("(o h) -> o h", o=1)), "ld", 16)
            b.inc(sync.dma_start(out=rows[:], in_=d_ones), "ld", 16)
            b.inc(sync.dma_start(out=bqn[0:24, :], in_=d_bqkv.rearrange("(t p) -> t p", p=128)),
                  "ld", 16)

        @blk.gpsimd
        def _(g):
            g.wait_ge(b.S("vz"), wz)
            b.inc(g.dma_start(out=wlfr[:], in_=d_wlif.rearrange("(o h) d -> o (h d)", o=1)), "ldg", 16)
            b.inc(g.dma_start(out=wqq[:].rearrange("p (kt n) -> p kt n", n=E),
                              in_=d_wqkv[:, 0:E].rearrange("(kt p) n -> p kt n", p=128)),
                  "ldg", 16)

            b.inc(g.dma_start(out=resid[:].rearrange("p (tt d) -> p tt d", d=E),
                              in_=d_x.rearrange("(tt p) d -> p tt d", p=128)),
                  "x", 16)

        w_ld = b.n("ld")

        @blk.vector
        def _(v):
            v.wait_ge(b.S("ld"), w_ld)
            v.wait_ge(b.S("ldg"), b.n("ldg"))
            v.tensor_copy(identb[:], idf[:])
            v.tensor_copy(ones1b[:], rows[0:1, 0:128])
            b.chain(v, v.transpose(tsc2[:], tsc[:]), "vch")
            b.chain(v, v.tensor_mul(hdt[:], hdt[:], wlf[:]), "vch")
            b.chain(v, v.reduce_sum(itmp[0:16, 0:1], hdt[:], axis=AX.X), "vch")
            b.chain(v, v.tensor_add(blife[0:16, :], tsc2[0:16, 0:1], itmp[0:16, 0:1]), "vch")
            v.tensor_copy(blife[32:48, :], blife[0:16, :])
            for c in range(4):
                v.transpose(bqkvc[c * 32:(c + 1) * 32, 0:32], bqn[0:32, c * 32:(c + 1) * 32])
            b.inc(v.memset(itmp[:], 0.0), "vp")

        @blk.vector
        def _(v):
            v.tensor_copy(wlfh[:], wlfr[:])
            b.chain(v, v.tensor_copy(wlft[:], wlfh[:]), "vch")
            v.tensor_sub(wlft[:], wlfr[:], wlft[:])
            b.inc(v.tensor_copy(wlfl[:], wlft[:]), "vp")

        for half in range(2):
            wv = b.n("vp")
            wvwb = b.n("vwb")

            @blk.tensor
            def _(tn, half=half, wv=wv, wvwb=wvwb):
                tn.wait_ge(b.S("vp"), wv)
                if half == 1:
                    tn.wait_ge(b.S("vwb"), wvwb)
                tn.matmul(ipb[:], ones1b[:], wlfh[0:1, half * 512:(half + 1) * 512],
                          start=True, stop=False)
                b.inc(tn.matmul(ipb[:], ones1b[:], wlfl[0:1, half * 512:(half + 1) * 512],
                                start=False, stop=True), "twb")

            wt = b.n("twb")

            @blk.vector
            def _(v, half=half, wt=wt):
                v.wait_ge(b.S("twb"), wt)
                b.inc(v.tensor_copy(wlbc[:, half * 512:(half + 1) * 512], ipb[:]), "vwb")

        wv = b.n("vwb")

        @blk.vector
        def _(v):
            v.wait_ge(b.S("vwb"), wv)
            for kt in range(8):
                v.tensor_mul(prodw[:], wqq[:, kt * E:(kt + 1) * E], wlbc[:])
                ins = None
                for h in range(H):
                    ins = v.reduce_sum(wqlf[:, kt * H + h:kt * H + h + 1],
                                       prodw[:, h * 64:(h + 1) * 64], axis=AX.X)
                b.chain(v, ins, "vch")
            wq2 = wqlif[:].rearrange("p (kt t h) -> p kt t h", t=2, h=H)
            b.chain(v, v.tensor_copy(wq2[:, :, 0, :], wqlf[:].rearrange("p (kt h) -> p kt h", h=H)), "vch")
            b.chain(v, v.tensor_copy(prodw[:, 0:8 * H], wq2[:, :, 0, :]), "vch")
            b.chain(v, v.tensor_sub(prodw[:, 0:8 * H], wqlf[:], prodw[:, 0:8 * H]), "vch")
            ins = v.tensor_copy(wq2[:, :, 1, :], prodw[:, 0:8 * H].rearrange("p (kt h) -> p kt h", h=H))
            b.chain(v, ins, "vch")
            b.inc(v.memset(itmp[:, 31:32], 0.0), "vp")
        winita = b.n("vp")
        inita.close()

        initb = ExitStack()
        stage = initb.enter_context(nc.sbuf_tensor("stage", [128, 2 * 2048], F32, side="right"))
        tquant = initb.enter_context(nc.sbuf_tensor("tquant", [128, 2048], BF16, side="right"))
        stq = initb.enter_context(nc.sbuf_tensor("stq", [128, 2048], F32, side="right"))
        acc = initb.enter_context(nc.sbuf_tensor("acc", [128, 36], F32, side="right"))
        mrow1 = initb.enter_context(nc.sbuf_tensor("mrow1", [1, 512], F32, side="right"))
        hrow = initb.enter_context(nc.sbuf_tensor("hrow", [1, 12], BF16, side="right"))
        ipt = initb.enter_context(nc.psum_tensor("ipt", [128, 1024], BF16, side="right"))
        ipb2 = initb.enter_context(nc.psum_tensor("ipb2", [128, 512], F32, side="right"))

        ffnw = ExitStack()
        tg = ffnw.enter_context(nc.sbuf_tensor("tg", [128, 8 * INNER], FP8))
        tu = ffnw.enter_context(nc.sbuf_tensor("tu", [128, 8 * INNER], FP8))
        td = ffnw.enter_context(nc.sbuf_tensor("td", [128, 32 * E], FP8))

        WSPEC = [(d_wg, 32, E, tg, INNER, 0), (d_wu, 32, E, tu, INNER, 1),
                 (d_wd, 8, INNER, td, E, 2)]

        @blk.vector
        def _(v):
            b.inc(v.memset(acc[:], 0.0), "vacz")

        wacz = b.n("vacz")

        def w_pieces(wid):
            return [(o, min(2048, wid - o)) for o in range(0, wid, 2048)]

        for wi, (dw, ntl, wid, dst, dcols, slot) in enumerate(WSPEC):
            for ot in range(ntl):
                for (po, pw) in w_pieces(wid):
                    par = (b.n("wld") // 16) % 2

                    first_wld = (b.n("wld") == 0)

                    @blk.sync
                    def _(sync, dw=dw, ot=ot, po=po, pw=pw, par=par, first_wld=first_wld):
                        if first_wld:
                            sync.wait_ge(b.S("vp"), winita)
                        if b.n("vabs") >= 2:
                            sync.wait_ge(b.S("vabs"), b.n("vabs") - 1)
                        b.inc(sync.dma_start(out=stage[:, par * 2048: par * 2048 + pw],
                                             in_=dw[ot * 128:(ot + 1) * 128, po:po + pw]),
                              "wld", 16)

                    wldn = b.n("wld")

                    @blk.vector
                    def _(v, wi=wi, pw=pw, par=par, wldn=wldn):
                        v.wait_ge(b.S("wld"), wldn)
                        if b.n("vabs") == 0:
                            v.wait_ge(b.S("vacz"), wacz)
                        b.chain(v, v.reduce_sum(acc[:, 35:36],
                                                stage[:, par * 2048: par * 2048 + pw],
                                                axis=AX.X, apply_absolute_value=True), "vch")
                        ins = v.tensor_add(acc[:, wi:wi + 1], acc[:, wi:wi + 1], acc[:, 35:36])
                        b.chain_inc(v, ins, "vabs")

        wabs = b.n("vabs")

        @blk.vector
        def _(v):
            v.wait_ge(b.S("vabs"), wabs)
            for wi in range(3):
                ins = None
                for c in range(4):
                    ins = v.transpose(cmat[0:32, c * 32:(c + 1) * 32],
                                      acc[c * 32:(c + 1) * 32, wi:wi + 32])
                b.chain(v, ins, "vch")
                b.chain(v, v.reduce_sum(mrow1[0:1, 384 + wi:385 + wi],
                                        cmat[0:1, 0:128], axis=AX.X), "vch")
            b.chain(v, v.tensor_scalar(mrow1[0:1, 390:393], mrow1[0:1, 384:387],
                                       1.0 / (INNER * E), 1e-5, op0=ALU.mult, op1=ALU.max), "vch")
            b.chain(v, v.reciprocal(mrow1[0:1, 393:396], mrow1[0:1, 390:393]), "vch")
            b.chain(v, v.reciprocal(mrow1[0:1, 396:399], mrow1[0:1, 393:396]), "vch")
            b.chain(v, v.tensor_copy(hrow[0:1, 0:3], mrow1[0:1, 393:396]), "vch")
            b.chain(v, v.tensor_copy(hrow[0:1, 6:9], mrow1[0:1, 396:399]), "vch")
            b.chain(v, v.tensor_copy(mrow1[0:1, 399:402], hrow[0:1, 0:3]), "vch")
            b.chain(v, v.tensor_copy(mrow1[0:1, 402:405], hrow[0:1, 6:9]), "vch")
            b.chain(v, v.tensor_sub(mrow1[0:1, 405:408], mrow1[0:1, 393:396],
                                    mrow1[0:1, 399:402]), "vch")
            b.chain(v, v.tensor_sub(mrow1[0:1, 408:411], mrow1[0:1, 396:399],
                                    mrow1[0:1, 402:405]), "vch")
            b.chain(v, v.tensor_copy(hrow[0:1, 3:6], mrow1[0:1, 405:408]), "vch")
            ins = v.tensor_copy(hrow[0:1, 9:12], mrow1[0:1, 408:411])
            b.chain(v, ins, "vch")
            b.inc(v.memset(acc[:, 34:35], 0.0), "vp")

        for wi in range(3):
            for which in range(2):
                wv = b.n("vp")
                wbcv = b.n("vbc")

                @blk.tensor
                def _(tn, wi=wi, which=which, wv=wv, wbcv=wbcv):
                    tn.wait_ge(b.S("vp"), wv)
                    if wbcv > 0:
                        tn.wait_ge(b.S("vbc"), wbcv)
                    chi = which * 6 + wi
                    tn.matmul(ipb2[:, 0:1], ones1b[:], hrow[0:1, chi:chi + 1],
                              start=True, stop=False)
                    b.inc(tn.matmul(ipb2[:, 0:1], ones1b[:], hrow[0:1, chi + 3:chi + 4],
                                    start=False, stop=True), "tbc")

                wt = b.n("tbc")

                @blk.vector
                def _(v, wi=wi, which=which, wt=wt):
                    v.wait_ge(b.S("tbc"), wt)
                    b.inc(v.tensor_copy(wcols[:, 3 * which + wi:3 * which + wi + 1],
                                        ipb2[:, 0:1]), "vbc")

        for wi, (dw, ntl, wid, dst, dcols, slot) in enumerate(WSPEC):
            for ot in range(ntl):
                for (po, pw) in w_pieces(wid):
                    par = (b.n("wld") // 16) % 2

                    @blk.sync
                    def _(sync, dw=dw, ot=ot, po=po, pw=pw, par=par):
                        if b.n("vq2") >= 2:
                            sync.wait_ge(b.S("vq2"), b.n("vq2") - 1)
                        else:
                            sync.wait_ge(b.S("vabs"), wabs)
                        b.inc(sync.dma_start(out=stage[:, par * 2048: par * 2048 + pw],
                                             in_=dw[ot * 128:(ot + 1) * 128, po:po + pw]),
                              "wld", 16)

                    wldn = b.n("wld")
                    wt8 = b.n("t8")

                    @blk.vector
                    def _(v, wi=wi, pw=pw, par=par, wldn=wldn, wt8=wt8):
                        v.wait_ge(b.S("wld"), wldn)
                        if wt8 > 0:
                            v.wait_ge(b.S("t8"), wt8)
                        st = stage[:, par * 2048: par * 2048 + pw]
                        sq = stq[:, 0:pw]
                        v.tensor_scalar(sq, st, wcols[:, wi:wi + 1], MAGIC,
                                        op0=ALU.mult, op1=ALU.add)
                        v.tensor_scalar(sq, sq, MAGIC, None, op0=ALU.subtract)
                        v.tensor_scalar(tquant[:, 0:pw], sq, -1.0, 1.0,
                                        op0=ALU.max, op1=ALU.min)
                        b.inc(v.memset(stq[:, 0:1], 0.0), "vq2")

                    nkb = pw // 128
                    for g0 in range(0, nkb, 4):
                        gn = min(4, nkb - g0)
                        wq2 = b.n("vq2")
                        ws8 = b.n("se8")

                        wbcf = b.n("vbc")

                        @blk.tensor
                        def _(tn, g0=g0, gn=gn, wq2=wq2, ws8=ws8, wbcf=wbcf):
                            tn.wait_ge(b.S("vq2"), wq2)
                            if ws8 > 0:
                                tn.wait_ge(b.S("se8"), ws8)
                            else:
                                tn.wait_ge(b.S("vbc"), wbcf)
                            ins = None
                            for i in range(gn):
                                ins = tn.transpose(ipt[:, i * 128:(i + 1) * 128],
                                                   tquant[:, (g0 + i) * 128:(g0 + i + 1) * 128],
                                                   identb[:])
                            b.inc(ins, "t8")

                        wt8b = b.n("t8")

                        @blk.scalar
                        def _(s, wi=wi, ot=ot, po=po, g0=g0, gn=gn, dst=dst, dcols=dcols,
                              wt8b=wt8b):
                            s.wait_ge(b.S("t8"), wt8b)
                            kb0 = po // 128 + g0
                            dv = dst[:].rearrange("p (k c) -> p k c", c=dcols)[
                                :, kb0:kb0 + gn, ot * 128:(ot + 1) * 128]
                            sv = ipt[:, 0:gn * 128].rearrange("p (g c) -> p g c", c=128)
                            b.inc(s.copy(dv, sv), "se8")

            wse = b.n("se8")

            @blk.scalar
            def _(s, dst=dst, slot=slot, wse=wse):
                s.wait_ge(b.S("se8"), wse)
                b.inc(s.dma_start(out=d_tern[slot], in_=dst[:]), "tsp", 16)

        if DEBUG:
            wv_d = b.n("vbc")
            wse_d = b.n("se8")

            @blk.sync
            def _(sync):
                sync.wait_ge(b.S("vbc"), wv_d)
                sync.wait_ge(b.S("se8"), wse_d)
                b.inc(sync.dma_start(out=d_dwc, in_=wcols[:]), "dbg", 16)
                b.inc(sync.dma_start(out=d_dtg, in_=tg[:, 0:4096]), "dbg", 16)
                b.inc(sync.dma_start(out=d_dtg2, in_=tg[:, 3 * 4096:4 * 4096]), "dbg", 16)
                b.inc(sync.dma_start(out=d_dtu, in_=tu[:]), "dbg", 16)
                b.inc(sync.dma_start(out=d_dtd, in_=td[:]), "dbg", 16)
                b.inc(sync.dma_start(out=d_dmr, in_=mrow1[:]), "dbg", 16)
                b.inc(sync.dma_start(out=d_dacc, in_=acc[:]), "dbg", 16)

        initb.close()

        @blk.scalar
        def _(s, wq2f=b.n("vq2")):
            s.wait_ge(b.S("vq2"), wq2f)

        @blk.tensor
        def _(tn, ws8f=b.n("se8"), wbcf=b.n("vbc")):
            tn.wait_ge(b.S("se8"), ws8f)
            tn.wait_ge(b.S("vbc"), wbcf)

        def open_ffn_scope(tagnum):
            sc = ExitStack()
            f = {}
            f["y1"] = sc.enter_context(nc.sbuf_tensor(f"y1_{tagnum}", [128, E], F32, side="right"))
            f["xq"] = sc.enter_context(nc.sbuf_tensor(f"xq_{tagnum}", [128, E], BF16, side="right"))
            f["xqT"] = sc.enter_context(nc.sbuf_tensor(f"xqT_{tagnum}", [128, 8 * 128], BF16, side="right"))
            nhb = 2 if tagnum == 2 else 1
            f["hbuf"] = [sc.enter_context(nc.sbuf_tensor(f"hbuf_{tagnum}_{i}", [128, INNER], BF16, side="right"))
                         for i in range(nhb)]
            f["sg"] = sc.enter_context(nc.sbuf_tensor(f"sg_{tagnum}", [128, 2 * 512],
                                                      BF16 if tagnum == 2 else F32, side="right"))
            f["qhb"] = sc.enter_context(nc.sbuf_tensor(f"qhb_{tagnum}", [128, 512], BF16, side="right"))
            f["qhT"] = sc.enter_context(nc.sbuf_tensor(f"qhT_{tagnum}", [128, 32 * 128], BF16, side="right"))
            if tagnum == 1:
                f["y2tt"] = sc.enter_context(nc.sbuf_tensor("y2tt", [128, 8 * 128], BF16, side="right"))
            else:
                f["outt"] = sc.enter_context(nc.sbuf_tensor("outt", [128, E], BF16, side="right"))
            f["pg"] = [sc.enter_context(nc.psum_tensor(f"pg{i}_{tagnum}", [128, 512], F32, side="right"))
                       for i in range(2)]
            f["pu"] = [sc.enter_context(nc.psum_tensor(f"pu{i}_{tagnum}", [128, 512], F32, side="right"))
                       for i in range(2)]
            f["pd"] = [sc.enter_context(nc.psum_tensor(f"pd{i}_{tagnum}", [128, 512], F32, side="right"))
                       for i in range(2)]
            f["pt"] = sc.enter_context(nc.psum_tensor(f"pt_{tagnum}", [128, 1024], BF16, side="right"))
            f["pi"] = sc.enter_context(nc.psum_tensor(f"pi_{tagnum}", [128, 512], F32, side="right"))
            return sc, f

        def emit_ln(rsl, ca, L, wr, y1, yfree):

            @blk.vector
            def _(v):
                if wr is not None:
                    v.wait_ge(wr[0], wr[1])
                b.chain_inc(v, v.reduce_sum(ca[:, 0:1], rsl, axis=AX.X), "vls" + L)

            wy1free = b.n(yfree)

            @blk.scalar
            def _(s):
                if wr is not None:
                    s.wait_ge(wr[0], wr[1])
                if wy1free > 0:
                    s.wait_ge(b.S(yfree), wy1free)
                b.inc(s.activation(y1[:], rsl, AF.Square), "sls" + L)

            wss = b.n("sls" + L)

            @blk.vector
            def _(v):
                v.wait_ge(b.S("sls" + L), wss)
                b.chain(v, v.reduce_sum(ca[:, 1:2], y1[:], axis=AX.X), "vch")
                b.chain(v, v.tensor_scalar(ca[:, 2:3], ca[:, 0:1], 1.0 / E, None,
                                           op0=ALU.mult), "vch")
                b.chain(v, v.tensor_mul(ca[:, 3:4], ca[:, 2:3], ca[:, 2:3]), "vch")
                b.chain(v, v.scalar_tensor_tensor(ca[:, 4:5], ca[:, 1:2], 1.0 / E,
                                                  ca[:, 3:4], op0=ALU.mult,
                                                  op1=ALU.subtract), "vch")
                ins = v.tensor_scalar(ca[:, 5:6], ca[:, 4:5], 1e-5, None, op0=ALU.add)
                b.chain_inc(v, ins, "vve" + L)

            wve = b.n("vve" + L)

            @blk.scalar
            def _(s):
                s.wait_ge(b.S("vve" + L), wve)
                b.chain(s, s.activation(ca[:, 6:7], ca[:, 5:6], AF.Ln), "sch")
                b.inc(s.activation(ca[:, 7:8], ca[:, 6:7], AF.Exp, scale=-0.5), "srs" + L)

            wrs = b.n("srs" + L)

            @blk.vector
            def _(v):
                v.wait_ge(b.S("srs" + L), wrs)
                b.inc(v.tensor_scalar(y1[:], rsl, ca[:, 2:3], ca[:, 7:8],
                                      op0=ALU.subtract, op1=ALU.mult), "vy1" + L)

        dbg_guard = {0: 0, 1: 0}
        pv = {1: {0: {}, 1: {}}, 2: {0: {}, 1: {}}}

        def emit_ffn_tile(tt, layer, f, resid_ready, gate_extra_waits=None,
                          emit_l1_extras=True, eo=None, phase=None):
            L = "L%d" % layer
            if (SERIAL_F1 if layer == 1 else SERIAL_F2):
                wb = b.n("bar")

                @blk.vector
                def _(v):
                    v.wait_ge(b.S("bar"), wb)

                @blk.scalar
                def _(s):
                    s.wait_ge(b.S("bar"), wb)

                @blk.tensor
                def _(tn):
                    tn.wait_ge(b.S("bar"), wb)
            tgn = str(layer)
            pp = (tt if eo is None else eo) % 2
            ca = colsA[:, 16 * pp:16 * (pp + 1)]
            rsl = resid[:, tt * E:(tt + 1) * E]
            y1, xq, xqT = f["y1"], f["xq"], f["xqT"]
            hbuf = f["hbuf"][pp % len(f["hbuf"])]
            sg, qhb, qhT = f["sg"], f["qhb"], f["qhT"]
            pg, pu, pd, pt, pi = f["pg"], f["pu"], f["pd"], f["pt"], f["pi"]

            if phase == "back":
                return emit_ffn_back(tt, layer, f, eo)
            yfree_sem = "vy2b" if layer == 1 else ("vxq" + L)
            emit_ln(rsl, ca, L, resid_ready, y1, yfree_sem)

            wtx = b.n("txq" + L)
            wty2 = b.n("ty2")

            wg_pp = dbg_guard[pp]

            @blk.vector
            def _(v):
                if DEBUG and layer == 1 and wg_pp > 0:
                    v.wait_ge(b.S("dbg"), wg_pp)
                b.chain(v, v.reduce_max(ca[:, 8:9], y1[:], axis=AX.X,
                                        apply_absolute_value=True), "vch")
                b.chain(v, v.tensor_scalar(ca[:, 9:10], ca[:, 8:9], 1e-5, None,
                                           op0=ALU.max), "vch")
                b.chain(v, v.tensor_scalar(ca[:, 10:11], ca[:, 9:10], 1.0 / 127.0, None,
                                           op0=ALU.mult), "vch")
                b.chain(v, v.reciprocal(ca[:, 11:12], ca[:, 9:10]), "vch")
                b.chain(v, v.tensor_scalar(ca[:, 11:12], ca[:, 11:12], 127.0, None,
                                           op0=ALU.mult), "vch")
                b.chain(v, v.tensor_mul(ca[:, 12:13], ca[:, 10:11], wcols[:, 3:4]), "vch")
                b.chain(v, v.tensor_mul(ca[:, 13:14], ca[:, 10:11], wcols[:, 4:5]), "vch")
                if wtx > 0:
                    v.wait_ge(b.S("txq" + L), wtx)
                if layer == 1 and wty2 > 0:
                    v.wait_ge(b.S("ty2"), wty2)
                v.tensor_scalar(y1[:], y1[:], ca[:, 11:12], MAGIC, op0=ALU.mult, op1=ALU.add)
                v.tensor_scalar(xq[:], y1[:], MAGIC, None, op0=ALU.subtract)
                b.inc(v.memset(ca[:, 15:16], 0.0), "vxq" + L)

            wxq = b.n("vxq" + L)
            if DEBUG and layer == 1:
                @blk.sync
                def _(sync):
                    sync.wait_ge(b.S("vxq" + L), wxq)
                    b.inc(sync.dma_start(out=d_dsnap[2 * tt], in_=ca[:, 8:16]), "dbg", 16)
            if DEBUG and tt == 0 and layer == 1:
                wy1d = b.n("vy1" + L)

                @blk.sync
                def _(sync):
                    sync.wait_ge(b.S("vy1" + L), wy1d)
                    b.inc(sync.dma_start(out=d_dy1, in_=y1[:]), "dbg", 16)

                @blk.sync
                def _(sync):
                    sync.wait_ge(b.S("vxq" + L), wxq)
                    b.inc(sync.dma_start(out=d_dxq, in_=xq[:]), "dbg", 16)

            for g0 in (0, 4):
                wfr = b.n("ptf" + L)

                @blk.tensor
                def _(tn, g0=g0, wfr=wfr):
                    if g0 == 0:
                        tn.wait_ge(b.S("vxq" + L), wxq)
                    if wfr > 0:
                        tn.wait_ge(b.S("ptf" + L), wfr)
                    ins = None
                    for i in range(4):
                        ins = tn.transpose(pt[:, i * 128:(i + 1) * 128],
                                           xq[:, (g0 + i) * 128:(g0 + i + 1) * 128], identb[:])
                    b.inc(ins, "txq" + L)

                wtx2 = b.n("txq" + L)
                wgu_prev = b.n("tgu" + L)

                @blk.scalar
                def _(s, g0=g0, wtx2=wtx2, wgu_prev=wgu_prev):
                    s.wait_ge(b.S("txq" + L), wtx2)
                    if wgu_prev > 0:
                        s.wait_ge(b.S("tgu" + L), wgu_prev)
                    s.copy(xqT[:, g0 * 128:(g0 + 4) * 128], pt[:, 0:512])
                    b.inc(s.copy(colsA[:, 30:31], colsA[:, 30:31]),
                          "ptf" + L)

            wxt = b.n("ptf" + L)
            if DEBUG and tt == 0 and layer == 1:
                @blk.sync
                def _(sync):
                    sync.wait_ge(b.S("ptf" + L), wxt)
                    b.inc(sync.dma_start(out=d_dxqt, in_=xqT[:]), "dbg", 16)

            for ncn in range(8):
                par = ncn % 2
                wfg = b.n("ssg" + L)
                wfu = b.n("vh" + L)

                @blk.tensor
                def _(tn, ncn=ncn, par=par, wfg=wfg, wfu=wfu):
                    if ncn == 0:
                        tn.wait_ge(b.S("ptf" + L), wxt)
                        if gate_extra_waits:
                            for snm, sval in gate_extra_waits:
                                if sval > 0:
                                    tn.wait_ge(b.S(snm), sval)
                    if wfg >= 2:
                        tn.wait_ge(b.S("ssg" + L), wfg - 1)
                    if wfu >= 2:
                        tn.wait_ge(b.S("vh" + L), wfu - 1)
                    for k in range(8):
                        tn.matmul(pg[par][:], xqT[:, k * 128:(k + 1) * 128],
                                  tg[:, k * INNER + ncn * 512: k * INNER + ncn * 512 + 512],
                                  start=(k == 0), stop=(k == 7))
                    ins = None
                    for k in range(8):
                        ins = tn.matmul(pu[par][:], xqT[:, k * 128:(k + 1) * 128],
                                        tu[:, k * INNER + ncn * 512: k * INNER + ncn * 512 + 512],
                                        start=(k == 0), stop=(k == 7))
                    b.inc(ins, "tgu" + L)

                wgu = b.n("tgu" + L)
                wsgfree = b.n("vh" + L)

                @blk.scalar
                def _(s, ncn=ncn, par=par, wgu=wgu, wsgfree=wsgfree):
                    s.wait_ge(b.S("tgu" + L), wgu)
                    if ncn == 0:
                        s.wait_ge(b.S("vxq" + L), wxq)
                    if wsgfree >= 2:
                        s.wait_ge(b.S("vh" + L), wsgfree - 1)
                    b.inc(s.activation(sg[:, par * 512:(par + 1) * 512], pg[par][:],
                                       AF.Silu, scale=ca[:, 12:13]), "ssg" + L)

                wsg = b.n("ssg" + L)

                @blk.vector
                def _(v, ncn=ncn, par=par, wsg=wsg):
                    v.wait_ge(b.S("ssg" + L), wsg)
                    b.inc(v.scalar_tensor_tensor(hbuf[:, ncn * 512:(ncn + 1) * 512],
                                                 pu[par][:], ca[:, 13:14],
                                                 sg[:, par * 512:(par + 1) * 512],
                                                 op0=ALU.mult, op1=ALU.mult), "vh" + L)

            if DEBUG and tt == 0 and layer == 1:
                wsg_d = b.n("ssg" + L)
                wh_d = b.n("vh" + L)

                @blk.sync
                def _(sync):
                    sync.wait_ge(b.S("ssg" + L), wsg_d - 7)
                    b.inc(sync.dma_start(out=d_dsg, in_=sg[:, 0:512]), "dbg", 16)

                @blk.sync
                def _(sync):
                    sync.wait_ge(b.S("vh" + L), wh_d)
                    b.inc(sync.dma_start(out=d_dh, in_=hbuf[:]), "dbg", 16)

            if phase == "front":
                return

            emit_ffn_back(tt, layer, f, eo)

        def emit_ffn_back(tt, layer, f, eo):
            L = "L%d" % layer
            pp = (tt if eo is None else eo) % 2
            ca = colsA[:, 16 * pp:16 * (pp + 1)]
            rsl = resid[:, tt * E:(tt + 1) * E]
            y1, xq, xqT = f["y1"], f["xq"], f["xqT"]
            hbuf = f["hbuf"][pp % len(f["hbuf"])]
            sg, qhb, qhT = f["sg"], f["qhb"], f["qhT"]
            pg, pu, pd, pt, pi = f["pg"], f["pu"], f["pd"], f["pt"], f["pi"]
            yfree_sem = "vy2b" if layer == 1 else ("vxq" + L)
            emit_l1_extras = (layer == 1)

            wg_xs = b.n("dbg")

            @blk.vector
            def _(v):
                if DEBUG and layer == 1 and wg_xs > 0:
                    v.wait_ge(b.S("dbg"), wg_xs)
                b.chain(v, v.reduce_max(ca[:, 8:9], hbuf[:], axis=AX.X,
                                        apply_absolute_value=True), "vch")
                b.chain(v, v.tensor_scalar(ca[:, 9:10], ca[:, 8:9], 1e-5, None,
                                           op0=ALU.max), "vch")
                b.chain(v, v.tensor_scalar(ca[:, 10:11], ca[:, 9:10], 1.0 / 127.0, None,
                                           op0=ALU.mult), "vch")
                b.chain(v, v.reciprocal(ca[:, 11:12], ca[:, 9:10]), "vch")
                b.chain(v, v.tensor_scalar(ca[:, 11:12], ca[:, 11:12], 127.0, None,
                                           op0=ALU.mult), "vch")
                b.chain(v, v.tensor_mul(ca[:, 14:15], ca[:, 10:11], wcols[:, 5:6]), "vch")
                b.chain(v, v.tensor_scalar(ca[:, 14:15], ca[:, 14:15], 0.5, None,
                                           op0=ALU.mult), "vch")


            if DEBUG and layer == 1:
                wq0 = b.n("vqh" + L) + 1

                @blk.sync
                def _(sync, wq0=wq0):
                    sync.wait_ge(b.S("vqh" + L), wq0)
                    b.inc(sync.dma_start(out=d_dsnap[2 * tt + 1], in_=ca[:, 8:16]), "dbg", 16)
                dbg_guard[pp] = b.n("dbg")

            for ncn in range(8):
                wqf = b.n("tqh" + L)

                @blk.vector
                def _(v, ncn=ncn, wqf=wqf):
                    if wqf > 0:
                        v.wait_ge(b.S("tqh" + L), wqf)
                    hs = hbuf[:, ncn * 512:(ncn + 1) * 512]
                    st = (sg[:, (ncn % 2) * 512:(ncn % 2 + 1) * 512] if layer == 1
                          else pd[ncn % 2][:])
                    v.tensor_scalar(st, hs, ca[:, 11:12], MAGIC, op0=ALU.mult, op1=ALU.add)
                    v.tensor_scalar(qhb[:], st, MAGIC, None, op0=ALU.subtract)
                    b.inc(v.memset(ca[:, 15:16], 0.0), "vqh" + L)

                wqh = b.n("vqh" + L)
                wqtf = b.n("ptf" + L)
                wtd_prev = b.n("td" + L)

                @blk.tensor
                def _(tn, ncn=ncn, wqh=wqh, wqtf=wqtf):
                    tn.wait_ge(b.S("vqh" + L), wqh)
                    if wqtf > 0:
                        tn.wait_ge(b.S("ptf" + L), wqtf)
                    ins = None
                    for i in range(4):
                        ins = tn.transpose(pt[:, i * 128:(i + 1) * 128],
                                           qhb[:, i * 128:(i + 1) * 128], identb[:])
                    b.inc(ins, "tqh" + L)

                wtq = b.n("tqh" + L)

                @blk.scalar
                def _(s, ncn=ncn, wtq=wtq, wtd_prev=wtd_prev):
                    s.wait_ge(b.S("tqh" + L), wtq)
                    if wtd_prev > 0 and ncn == 0:
                        s.wait_ge(b.S("td" + L), wtd_prev)
                    s.copy(qhT[:, ncn * 512:(ncn + 1) * 512], pt[:, 0:512])
                    b.inc(s.copy(colsA[:, 30:31], colsA[:, 30:31]),
                          "ptf" + L)

            wqt = b.n("ptf" + L)

            for n2 in range(2):
                wfd = b.n("vr" + L)

                @blk.tensor
                def _(tn, n2=n2, wfd=wfd):
                    if n2 == 0:
                        tn.wait_ge(b.S("ptf" + L), wqt)
                    if wfd >= 2:
                        tn.wait_ge(b.S("vr" + L), wfd - 1)
                    ins = None
                    for k in range(32):
                        ins = tn.matmul(pd[n2][:], qhT[:, k * 128:(k + 1) * 128],
                                        td[:, k * E + n2 * 512: k * E + n2 * 512 + 512],
                                        start=(k == 0), stop=(k == 31))
                    b.inc(ins, "td" + L)

                wtd = b.n("td" + L)
                wof = b.n("out")

                @blk.vector
                def _(v, n2=n2, wtd=wtd, wof=wof):
                    v.wait_ge(b.S("td" + L), wtd)
                    if layer == 1:
                        b.inc(v.scalar_tensor_tensor(rsl[:, n2 * 512:(n2 + 1) * 512],
                                                     pd[n2][:], ca[:, 14:15],
                                                     rsl[:, n2 * 512:(n2 + 1) * 512],
                                                     op0=ALU.mult, op1=ALU.add), "vr" + L)
                    else:
                        outt = f["outt"]
                        if wof >= 16:
                            v.wait_ge(b.S("out"), wof)
                        v.scalar_tensor_tensor(
                            outt[:, n2 * 512:(n2 + 1) * 512],
                            pd[n2][:], ca[:, 14:15], rsl[:, n2 * 512:(n2 + 1) * 512],
                            op0=ALU.mult, op1=ALU.add)
                        b.inc(v.memset(ca[:, 15:16], 0.0), "vr" + L)

            if layer == 2:
                wvr = b.n("vr" + L)

                @blk.gpsimd
                def _(g):
                    g.wait_ge(b.S("vr" + L), wvr)
                    b.inc(g.dma_start(out=d_out[tt * 128:(tt + 1) * 128, :],
                                      in_=f["outt"][:]), "out", 16)

            if layer == 1 and emit_l1_extras:
                y2tt = f["y2tt"]
                wvr = b.n("vr" + L)
                emit_ln(rsl, ca, "M", (b.S("vr" + L), wvr), y1, "vxq" + L)
                wy2f = b.n("ty2")
                wtx3 = b.n("txq" + L)

                @blk.vector
                def _(v):
                    if wy2f > 0:
                        v.wait_ge(b.S("ty2"), wy2f)
                    if wtx3 > 0:
                        v.wait_ge(b.S("txq" + L), wtx3)
                    v.tensor_copy(xq[:], y1[:])
                    b.inc(v.memset(ca[:, 15:16], 0.0), "vy2b")

                wy2 = b.n("vy2b")
                for g0 in (0, 4):
                    wfr = b.n("ptf" + L)
                    wti_prev = b.n("ti")
                    wy2d_prev = b.n("y2d")

                    @blk.tensor
                    def _(tn, g0=g0, wfr=wfr):
                        if g0 == 0:
                            tn.wait_ge(b.S("vy2b"), wy2)
                        if wfr > 0:
                            tn.wait_ge(b.S("ptf" + L), wfr)
                        ins = None
                        for i in range(4):
                            ins = tn.transpose(pt[:, i * 128:(i + 1) * 128],
                                               xq[:, (g0 + i) * 128:(g0 + i + 1) * 128],
                                               identb[:])
                        b.inc(ins, "ty2")

                    wt2 = b.n("ty2")

                    @blk.scalar
                    def _(s, g0=g0, wt2=wt2, wti_prev=wti_prev, wy2d_prev=wy2d_prev):
                        s.wait_ge(b.S("ty2"), wt2)
                        if g0 == 0:
                            if wti_prev > 0:
                                s.wait_ge(b.S("ti"), wti_prev)
                            if wy2d_prev > 0:
                                s.wait_ge(b.S("y2d"), wy2d_prev)
                        s.copy(y2tt[:, g0 * 128:(g0 + 4) * 128], pt[:, 0:512])
                        b.inc(s.copy(colsA[:, 30:31], colsA[:, 30:31]),
                              "ptf" + L)

                wsy = b.n("ptf" + L)
                bb, jc = tt // 16, tt % 16

                @blk.scalar
                def _(s):
                    s.wait_ge(b.S("ptf" + L), wsy)
                    b.inc(s.dma_start(
                        out=d_y2t[bb, :, jc * 128:(jc + 1) * 128]
                        .rearrange("(kt p) n -> p kt n", p=128),
                        in_=y2tt[:].rearrange("p (kt n) -> p kt n", n=128)), "y2d", 16)

                wpi = b.n("vi")

                @blk.tensor
                def _(tn):
                    tn.wait_ge(b.S("ptf" + L), wsy)
                    if wpi > 0:
                        tn.wait_ge(b.S("vi"), wpi)
                    ins = None
                    for k in range(8):
                        for t2 in range(2):
                            ins = tn.matmul(pi[:, 0:16], y2tt[:, k * 128:(k + 1) * 128],
                                            wqlif[:, k * 2 * H + t2 * H: k * 2 * H + (t2 + 1) * H],
                                            start=(k == 0 and t2 == 0), stop=False)
                    b.inc(ins, "ti")

                wtyh = b.n("ty2")

                @blk.vector
                def _(v):
                    v.wait_ge(b.S("ty2"), wtyh)
                    v.tensor_sub(xq[:], y1[:], xq[:])
                    b.inc(v.memset(ca[:, 15:16], 0.0), "vy2b")

                wy2l = b.n("vy2b")
                wtih = b.n("ti")
                wy2dt = b.n("y2d")
                for g0 in (0, 4):
                    wfr = b.n("ptf" + L)

                    @blk.tensor
                    def _(tn, g0=g0, wfr=wfr):
                        if g0 == 0:
                            tn.wait_ge(b.S("vy2b"), wy2l)
                        if wfr > 0:
                            tn.wait_ge(b.S("ptf" + L), wfr)
                        ins = None
                        for i in range(4):
                            ins = tn.transpose(pt[:, i * 128:(i + 1) * 128],
                                               xq[:, (g0 + i) * 128:(g0 + i + 1) * 128],
                                               identb[:])
                        b.inc(ins, "ty2")

                    wt2 = b.n("ty2")

                    @blk.scalar
                    def _(s, g0=g0, wt2=wt2):
                        s.wait_ge(b.S("ty2"), wt2)
                        if g0 == 0:
                            s.wait_ge(b.S("ti"), wtih)
                            s.wait_ge(b.S("y2d"), wy2dt)
                        s.copy(y2tt[:, g0 * 128:(g0 + 4) * 128], pt[:, 0:512])
                        b.inc(s.copy(colsA[:, 30:31], colsA[:, 30:31]),
                              "ptf" + L)

                wsyl = b.n("ptf" + L)

                @blk.tensor
                def _(tn):
                    tn.wait_ge(b.S("ptf" + L), wsyl)
                    ins = None
                    for k in range(8):
                        ins = tn.matmul(pi[:, 0:16], y2tt[:, k * 128:(k + 1) * 128],
                                        wqlif[:, k * 2 * H: k * 2 * H + H],
                                        start=False, stop=(k == 7))
                    b.inc(ins, "ti")

                wti = b.n("ti")

                @blk.vector
                def _(v):
                    v.wait_ge(b.S("ti"), wti)
                    ins = v.tensor_copy(itmp[:, 0:16], pi[:, 0:16])
                    b.chain_inc(v, ins, "vi")
                    base = 32 * bb
                    co = (jc % 2) * 128
                    ins = None
                    for c in range(4):
                        ins = v.transpose(scanIc[base:base + 32, co + c * 32: co + (c + 1) * 32],
                                          itmp[c * 32:(c + 1) * 32, 0:32])
                    b.chain(v, ins, "vch")
                    b.chain(v, v.tensor_scalar(scanIc[base:base + 16, co:co + 128],
                                               scanIc[base:base + 16, co:co + 128],
                                               blife[base:base + 16, 0:1], None,
                                               op0=ALU.add), "vch")


        def finish_tile(layer, pp, yfree_sem):
            L = "L%d" % layer
            yv = b.n(yfree_sem) if layer == 1 else b.n("vqh" + L)
            pv[layer][pp] = {"yfree": yv, "txq" + L: b.n("txq" + L),
                             "ty2": b.n("ty2"), "tgu" + L: b.n("tgu" + L)}

        def emit_tile_barrier(fb, flag=True):
            if flag:
                @blk.vector
                def _(v):
                    b.inc(v.engine_nop(), "bar")

                @blk.scalar
                def _(s):
                    b.inc(s.copy(colsA[:, 30:31], colsA[:, 30:31]), "bar")

                @blk.tensor
                def _(tn):
                    b.inc(tn.transpose(fb["pt"][:, 0:128], identb[:], identb[:]), "bar")

        ffn1sc, f1 = open_ffn_scope(1)
        wx = b.n("x")
        gate_waits = [("se8", b.n("se8")), ("vbc", b.n("vbc")), ("vp", winita)]
        pending = []
        for jc in range(16):
            for bb in (0, 1):
                tt = bb * 16 + jc
                emit_ffn_tile(tt, 1, f1, (b.S("x"), wx),
                              gate_extra_waits=(gate_waits if tt == 0 else None),
                              eo=2 * jc + bb)
                emit_tile_barrier(f1, SERIAL_F1)
                for _ in range(2):
                    if pending:
                        emit_scan_part(*pending.pop(0))
            pending += [(jc, p) for p in range(4)]
        for args in pending:
            emit_scan_part(*args)

        wcnt = b.n("vcnt")
        if STAGE < 2:
            @blk.gpsimd
            def _(g):
                g.wait_ge(b.S("vcnt"), wcnt)
                b.inc(g.dma_start(out=d_dbg, in_=counts[:]), "outg", 16)

            for tt in range(NT):
                wvr = b.n("vrL1")
                wof = b.n("out")

                @blk.vector
                def _(v, tt=tt, wvr=wvr, wof=wof):
                    v.wait_ge(b.S("vrL1"), wvr)
                    if wof >= 48:
                        v.wait_ge(b.S("out"), wof - 32)
                    b.inc(v.tensor_copy(f1["hbuf"][:, (tt % 4) * E:((tt % 4) + 1) * E],
                                        resid[:, tt * E:(tt + 1) * E]), "vod")

                wvo = b.n("vod")

                @blk.scalar
                def _(s, tt=tt, wvo=wvo):
                    s.wait_ge(b.S("vod"), wvo)
                    b.inc(s.dma_start(out=d_out[tt * 128:(tt + 1) * 128, :],
                                      in_=f1["hbuf"][:, (tt % 4) * E:((tt % 4) + 1) * E]),
                          "out", 16)

            @blk.scalar
            def _(s, wfin=b.n("out"), wy2fin=b.n("y2d"), wtsp=b.n("tsp"),
                  wdbg=b.n("dbg"), wog=b.n("outg")):
                s.wait_ge(b.S("out"), wfin)
                s.wait_ge(b.S("y2d"), wy2fin)
                s.wait_ge(b.S("tsp"), wtsp)
                if wog:
                    s.wait_ge(b.S("outg"), wog)
                if wdbg:
                    s.wait_ge(b.S("dbg"), wdbg)

        ffn1sc.close()
        ffnw.close()

        resid_ready_attn = [None] * NT
        if STAGE >= 2:
            attn = ExitStack()
            ae = attn.enter_context
            amask_t = ae(nc.sbuf_tensor("amask_t", [128, 16 * H], F32, side="right"))
            ind_tb = ae(nc.sbuf_tensor("ind_tb", [128, 16 * H], BF16, side="right"))
            amask_c = ae(nc.sbuf_tensor("amask_c", [NCK, H], F32, side="right"))
            ind_cb = ae(nc.sbuf_tensor("ind_cb", [NCK, H], BF16, side="right"))
            band01b = ae(nc.sbuf_tensor("band01b", [128, 512], BF16, side="right"))
            dedb = ae(nc.sbuf_tensor("dedb", [NCK, 16 * 128], BF16, side="right"))
            mrow = ae(nc.sbuf_tensor("mrow", [16, SQ], F32, side="right"))
            afl = ae(nc.sbuf_tensor("afl", [128, 1], F32, side="right"))
            mct = ae(nc.sbuf_tensor("mct", [16, NCK], BF16, side="right"))
            y2tb = ae(nc.sbuf_tensor("y2tb", [128, 8 * SQ], BF16, side="right"))
            wq_q = ae(nc.sbuf_tensor("wq_q", [128, 8 * 128], BF16, side="right"))
            wq_k = ae(nc.sbuf_tensor("wq_k", [128, 8 * 128], BF16, side="right"))
            wq_v = ae(nc.sbuf_tensor("wq_v", [128, 8 * 128], BF16, side="right"))
            qT2 = ae(nc.sbuf_tensor("qT2", [128, SQ], BF16, side="right"))
            kT2 = ae(nc.sbuf_tensor("kT2", [128, SQ], BF16, side="right"))
            vT2 = ae(nc.sbuf_tensor("vT2", [128, SQ], BF16, side="right"))
            vnat = ae(nc.sbuf_tensor("vnat", [128, SQ], BF16, side="right"))
            kc = ae(nc.sbuf_tensor("kc", [128, NCK], BF16, side="right"))
            vct = ae(nc.sbuf_tensor("vct", [128, NCK], BF16, side="right"))
            vc = ae(nc.sbuf_tensor("vc", [NCK, 128], BF16, side="right"))
            ebuf = ae(nc.sbuf_tensor("ebuf", [128, 2 * SQ], BF16, side="right"))
            ec = ae(nc.sbuf_tensor("ec", [NCK, SQ], BF16, side="right"))
            aobuf = ae(nc.sbuf_tensor("aobuf", [128, SQ], BF16, side="right"))
            attnT = ae(nc.sbuf_tensor("attnT", [128, 8 * SQ], BF16, side="right"))
            mrow2 = aobuf

            woutb = ae(nc.sbuf_tensor("woutb", [128, 8 * E], BF16, side="right"))
            boutb = ae(nc.sbuf_tensor("boutb", [1, E], BF16, side="right"))
            dmat = ae(nc.sbuf_tensor("dmat", [128, 16], F32, side="right"))
            rmat = ae(nc.sbuf_tensor("rmat", [128, 16], F32, side="right"))
            zcol = ae(nc.sbuf_tensor("zcol", [128, 16], F32, side="right"))
            pqk = [ae(nc.psum_tensor(f"pqk{i}", [128, 512], F32, side="right")) for i in range(2)]
            psc = [ae(nc.psum_tensor(f"psc{i}", [128, 512], F32, side="right")) for i in range(2)]
            pcc = ae(nc.psum_tensor("pcc", [NCK, 512], F32, side="right"))
            pdn = ae(nc.psum_tensor("pdn", [128, 512], F32, side="right"))
            pav = ae(nc.psum_tensor("pav", [128, 512], F32, side="right"))
            ptA = ae(nc.psum_tensor("ptA", [128, 1024], BF16, side="right"))

            BAR = [("ti", b.n("ti")), ("tdL1", b.n("tdL1")), ("ptfL1", b.n("ptfL1")),
                   ("vrL1", b.n("vrL1")), ("y2d", b.n("y2d")), ("vi", b.n("vi")),
                   ("vcnt", wcnt), ("tsp", b.n("tsp")), ("tguL1", b.n("tguL1"))]

            @blk.gpsimd
            def _(g):
                for nm, val in BAR:
                    if val > 0:
                        g.wait_ge(b.S(nm), val)
                b.inc(g.dma_start(out=d_ccin, in_=counts[:]), "ccg", 16)
                g.wait_ge(b.S("ccg"), 16)
                b.inc(g.collective_compute("AllReduce", ALU.add,
                                           replica_groups=[[0, 1, 2, 3, 4, 5, 6, 7]],
                                           ins=[d_ccin.opt()], outs=[d_ccout.opt()]), "ccg", 1)
                g.wait_ge(b.S("ccg"), 17)
                b.inc(g.dma_start(out=band01b[:], in_=d_band), "abl", 16)
                b.inc(g.dma_start(out=dedb[:], in_=d_ded), "abl", 16)
                b.inc(g.dma_start(out=woutb[:].rearrange("p (kt n) -> p kt n", n=E),
                                  in_=d_wout.rearrange("(kt p) n -> p kt n", p=128)), "abl", 16)
                b.inc(g.dma_start(out=boutb[:], in_=d_bout.rearrange("(o e) -> o e", o=1)),
                      "abl", 16)

            @blk.sync
            def _(sync):
                sync.wait_ge(b.S("ccg"), 17)
                b.inc(sync.dma_start(out=mrow[:], in_=d_ccout), "cbh", 16)

            wccb = b.n("cbh")

            @blk.vector
            def _(v):
                v.wait_ge(b.S("cbh"), wccb)
                b.chain(v, v.tensor_scalar(mrow2[0:16, :], mrow[:], 0.0, None,
                                           op0=ALU.is_gt), "vch")
                v.tensor_copy(mct[:, 0:4], mrow2[0:16, 0:4])
                v.tensor_copy(mct[:, 4:35],
                              mrow2[0:16, :].rearrange("h (n s) -> h n s", s=64)[:, 1:32, 0:1])
                b.chain(v, v.memset(mct[:, 35:40], 0.0), "vch")
                b.inc(v.memset(afl[:], 0.0), "vmk")

            wmk = b.n("vmk")
            for jt in range(17):
                wvm = b.n("vmv")

                @blk.tensor
                def _(tn, jt=jt, wvm=wvm):
                    if jt == 0:
                        tn.wait_ge(b.S("vmk"), wmk)
                    if wvm > 0:
                        tn.wait_ge(b.S("vmv"), wvm)
                    if jt < 16:
                        b.inc(tn.transpose(ptA[:, 0:16],
                                           mrow2[0:16, jt * 128:(jt + 1) * 128],
                                           identb[0:16, 0:16]), "tmk")
                    else:
                        b.inc(tn.transpose(ptA[0:NCK, 0:16], mct[:], identb[0:16, 0:16]),
                              "tmk")

                wtm = b.n("tmk")

                @blk.vector
                def _(v, jt=jt, wtm=wtm):
                    v.wait_ge(b.S("tmk"), wtm)
                    if jt < 16:
                        v.tensor_copy(ind_tb[:, jt * H:(jt + 1) * H], ptA[:, 0:16])
                        v.tensor_scalar(amask_t[:, jt * H:(jt + 1) * H], ptA[:, 0:16],
                                        1.0, NEG, op0=ALU.is_lt, op1=ALU.mult)
                    else:
                        v.tensor_copy(ind_cb[0:NCK, :], ptA[0:NCK, 0:16])
                        v.tensor_scalar(amask_c[0:NCK, :], ptA[0:NCK, 0:16],
                                        1.0, NEG, op0=ALU.is_lt, op1=ALU.mult)
                    b.inc(v.memset(afl[:], 0.0), "vmv")

            wvmv = b.n("vmv")
            wabl = b.n("abl")

            for bb in range(2):
                wy2p = b.n("tqk")

                @blk.sync
                def _(sync, bb=bb, wy2p=wy2p):
                    for nm, val in BAR:
                        if val > 0:
                            sync.wait_ge(b.S(nm), val)
                    if wy2p > 0:
                        sync.wait_ge(b.S("tqk"), wy2p)
                    b.inc(sync.dma_start(out=y2tb[:].rearrange("p (kt n) -> p kt n", n=SQ),
                                         in_=d_y2t[bb].rearrange("(kt p) n -> p kt n", p=128)),
                          "ay", 16)

                way = b.n("ay")
                for hp in range(8):
                    wfree = b.n("tqk")

                    @blk.gpsimd
                    def _(g, bb=bb, hp=hp, wfree=wfree):
                        if wfree > 0:
                            g.wait_ge(b.S("tqk"), wfree)
                        for dst, c0 in ((wq_q, hp * 128), (wq_k, E + hp * 128),
                                        (wq_v, 2 * E + hp * 128)):
                            b.inc(g.dma_start(
                                out=dst[:].rearrange("p (kt n) -> p kt n", n=128),
                                in_=d_wqkv[:, c0:c0 + 128]
                                .rearrange("(kt p) n -> p kt n", p=128)), "aw", 16)

                    waw = b.n("aw")
                    for pi3, (dstT, wsl, bcol) in enumerate(
                            ((qT2, wq_q, hp), (kT2, wq_k, 8 + hp), (vT2, wq_v, 16 + hp))):
                        for nt in range(4):
                            par = nt % 2
                            wev = b.n("vqk")

                            @blk.tensor
                            def _(tn, bb=bb, pi3=pi3, nt=nt, par=par, wsl=wsl, wev=wev):
                                if pi3 == 0 and nt == 0:
                                    tn.wait_ge(b.S("ay"), way)
                                    tn.wait_ge(b.S("aw"), waw)
                                if wev >= 2:
                                    tn.wait_ge(b.S("vqk"), wev - 1)
                                ins = None
                                for kt in range(8):
                                    ins = tn.matmul(pqk[par][:],
                                                    wsl[:, kt * 128:(kt + 1) * 128],
                                                    y2tb[:, kt * SQ + nt * 512: kt * SQ + nt * 512 + 512],
                                                    start=(kt == 0), stop=(kt == 7))
                                b.inc(ins, "tqk")

                            wq1 = b.n("tqk")
                            wwar = (b.n("tsc"), b.n("tcc"), b.n("tvt"), b.n("tdn"))

                            @blk.vector
                            def _(v, dstT=dstT, pi3=pi3, nt=nt, par=par, bcol=bcol,
                                  wq1=wq1, wwar=wwar):
                                v.wait_ge(b.S("tqk"), wq1)
                                if pi3 == 0 and nt == 0:
                                    for nm, val in zip(("tsc", "tcc", "tvt", "tdn"), wwar):
                                        if val > 0:
                                            v.wait_ge(b.S(nm), val)
                                v.tensor_scalar(dstT[:, nt * 512:(nt + 1) * 512],
                                                pqk[par][:], bqkvc[:, bcol:bcol + 1],
                                                None, op0=ALU.add)
                                b.inc(v.memset(afl[:], 0.0), "vqk")

                    wvq = b.n("vqk")
                    for g4 in range(4):
                        wsv = b.n("svt")

                        @blk.tensor
                        def _(tn, g4=g4, wsv=wsv):
                            if g4 == 0:
                                tn.wait_ge(b.S("vqk"), wvq)
                            if wsv > 0:
                                tn.wait_ge(b.S("svt"), wsv)
                            ins = None
                            for i in range(4):
                                jt = g4 * 4 + i
                                ins = tn.transpose(ptA[:, i * 128:(i + 1) * 128],
                                                   vT2[:, jt * 128:(jt + 1) * 128], identb[:])
                            b.inc(ins, "tvt")

                        wtv = b.n("tvt")

                        wavp = b.n("tav")

                        @blk.scalar
                        def _(s, g4=g4, wtv=wtv, wavp=wavp):
                            s.wait_ge(b.S("tvt"), wtv)
                            if g4 == 0 and wavp > 0:
                                s.wait_ge(b.S("tav"), wavp)
                            s.copy(vnat[:, g4 * 512:(g4 + 1) * 512], ptA[:, 0:512])
                            b.inc(s.copy(afl[:], afl[:]), "svt")

                    wkcc = (b.n("tcc"), b.n("tvc"))

                    @blk.vector
                    def _(v, wkcc=wkcc):
                        for nm, val in zip(("tcc", "tvc"), wkcc):
                            if val > 0:
                                v.wait_ge(b.S(nm), val)
                        v.tensor_copy(kc[:, 0:4], kT2[:, 0:4])
                        v.tensor_copy(kc[:, 4:35],
                                      kT2[:].rearrange("p (n s) -> p n s", s=64)[:, 1:32, 0:1])
                        v.memset(kc[:, 35:40], 0.0)
                        v.tensor_copy(vct[:, 0:4], vT2[:, 0:4])
                        v.tensor_copy(vct[:, 4:35],
                                      vT2[:].rearrange("p (n s) -> p n s", s=64)[:, 1:32, 0:1])
                        b.inc(v.memset(vct[:, 35:40], 0.0), "vkc")

                    wkc = b.n("vkc")
                    wsv2 = b.n("svt")

                    @blk.tensor
                    def _(tn, wkc=wkc, wsv2=wsv2):
                        tn.wait_ge(b.S("vkc"), wkc)
                        tn.wait_ge(b.S("svt"), wsv2)
                        b.inc(tn.transpose(ptA[0:NCK, 0:128], vct[:, 0:NCK], identb[:]), "tvc")

                    wvc = b.n("tvc")

                    wavp2 = b.n("tav")

                    @blk.scalar
                    def _(s, wvc=wvc, wavp2=wavp2):
                        s.wait_ge(b.S("tvc"), wvc)
                        if wavp2 > 0:
                            s.wait_ge(b.S("tav"), wavp2)
                        s.copy(vc[0:NCK, :], ptA[0:NCK, 0:128])
                        b.inc(s.copy(afl[:], afl[:]), "svt")

                    wtao_p = b.n("tao")
                    for hh in range(2):
                        h = hp * 2 + hh
                        hb = hh * 64
                        for qc in range(4):
                            wse = b.n("sec")

                            @blk.tensor
                            def _(tn, qc=qc, hb=hb, wse=wse):
                                if qc == 0:
                                    tn.wait_ge(b.S("svt"), b.n("svt"))
                                    if b.n("tav") > 0:
                                        tn.wait_ge(b.S("tav"), b.n("tav"))
                                if wse > 0:
                                    tn.wait_ge(b.S("sec"), wse)
                                b.inc(tn.matmul(pcc[0:NCK, :], kc[hb:hb + 64, 0:NCK],
                                                qT2[hb:hb + 64, qc * 512:(qc + 1) * 512],
                                                start=True, stop=True), "tcc")

                            wcc2 = b.n("tcc")

                            @blk.scalar
                            def _(s, qc=qc, h=h, wcc2=wcc2):
                                s.wait_ge(b.S("tcc"), wcc2)
                                if qc == 0:
                                    s.wait_ge(b.S("vmv"), wvmv)
                                b.inc(s.activation(ec[0:NCK, qc * 512:(qc + 1) * 512],
                                                   pcc[0:NCK, :], AF.Exp, scale=0.125,
                                                   bias=amask_c[0:NCK, h:h + 1]), "sec")

                            wse2 = b.n("sec")

                            @blk.vector
                            def _(v, qc=qc, wse2=wse2):
                                v.wait_ge(b.S("sec"), wse2)
                                if qc == 0:
                                    v.wait_ge(b.S("abl"), wabl)
                                v.tensor_mul(ec[0:NCK, qc * 512:(qc + 1) * 512],
                                             ec[0:NCK, qc * 512:(qc + 1) * 512],
                                             dedb[0:NCK, qc * 512:(qc + 1) * 512])
                                b.inc(v.memset(afl[:], 0.0), "vec")

                        for qt in range(16):
                            par = qt % 2
                            jts = []
                            if qt > 0:
                                va = 1 if qt == 1 else 0
                                jts.append((qt - 1, 0, va))
                            vbv = 3 if qt == 0 else 2
                            jts.append((qt, 128, vbv))
                            wex = b.n("sex")

                            @blk.tensor
                            def _(tn, qt=qt, par=par, jts=jts, hb=hb, wex=wex):
                                if wex >= 2:
                                    tn.wait_ge(b.S("sex"), wex - 1)
                                ins = None
                                for (jt, co, va) in jts:
                                    ins = tn.matmul(psc[par][:, co:co + 128],
                                                    kT2[hb:hb + 64, jt * 128:(jt + 1) * 128],
                                                    qT2[hb:hb + 64, qt * 128:(qt + 1) * 128],
                                                    start=True, stop=True)
                                b.inc(ins, "tsc")

                            wts = b.n("tsc")

                            wavh = b.n("tav")

                            @blk.scalar
                            def _(s, qt=qt, par=par, jts=jts, h=h, wts=wts, wavh=wavh):
                                s.wait_ge(b.S("tsc"), wts)
                                if qt == 0:
                                    s.wait_ge(b.S("vmv"), wvmv)
                                if qt == 0 and wavh > 0:
                                    s.wait_ge(b.S("tav"), wavh)
                                ins = None
                                for (jt, co, va) in jts:
                                    ins = s.activation(ebuf[:, qt * 256 + co: qt * 256 + co + 128],
                                                       psc[par][:, co:co + 128], AF.Exp,
                                                       scale=0.125,
                                                       bias=amask_t[:, jt * H + h: jt * H + h + 1])
                                b.inc(ins, "sex")

                            wsx = b.n("sex")

                            @blk.vector
                            def _(v, qt=qt, jts=jts, wsx=wsx):
                                v.wait_ge(b.S("sex"), wsx)
                                if qt == 0:
                                    v.wait_ge(b.S("abl"), wabl)
                                ins = None
                                for (jt, co, va) in jts:
                                    v.tensor_mul(
                                        ebuf[:, qt * 256 + co: qt * 256 + co + 128],
                                        ebuf[:, qt * 256 + co: qt * 256 + co + 128],
                                        band01b[:, va * 128:(va + 1) * 128])
                                b.inc(v.memset(afl[:], 0.0), "vbm")

                        wbm = b.n("vbm")
                        wvec = b.n("vec")
                        wvdm = b.n("vdm")

                        @blk.tensor
                        def _(tn, h=h, wbm=wbm, wvec=wvec, wvdm=wvdm):
                            tn.wait_ge(b.S("vbm"), wbm)
                            tn.wait_ge(b.S("vec"), wvec)
                            if wvdm > 0:
                                tn.wait_ge(b.S("vdm"), wvdm)
                            ins = None
                            for qt in range(16):
                                jts2 = ([(qt - 1, 0)] if qt > 0 else []) + [(qt, 128)]
                                first = True
                                for (jt, co) in jts2:
                                    tn.matmul(pdn[:, qt:qt + 1],
                                              ebuf[:, qt * 256 + co: qt * 256 + co + 128],
                                              ind_tb[:, jt * H + h: jt * H + h + 1],
                                              start=first, stop=False)
                                    first = False
                                ins = tn.matmul(pdn[:, qt:qt + 1],
                                                ec[0:NCK, qt * 128:(qt + 1) * 128],
                                                ind_cb[0:NCK, h:h + 1],
                                                start=False, stop=True)
                            b.inc(ins, "tdn")

                        wdn = b.n("tdn")

                        @blk.vector
                        def _(v, wdn=wdn):
                            v.wait_ge(b.S("tdn"), wdn)
                            b.inc(v.tensor_copy(dmat[:], pdn[:, 0:16]), "vdm")

                        wvd = b.n("vdm")

                        @blk.vector
                        def _(v, wvd=wvd):
                            v.wait_ge(b.S("vdm"), wvd)
                            b.chain(v, v.tensor_scalar(zcol[:], dmat[:], 0.0, None,
                                                       op0=ALU.is_le), "vch")
                            b.chain(v, v.tensor_add(dmat[:], dmat[:], zcol[:]), "vch")
                            b.inc(v.reciprocal(rmat[:], dmat[:]), "vrm")

                        wrm = b.n("vrm")
                        for half in range(2):
                            wao = b.n("vao")

                            @blk.tensor
                            def _(tn, half=half, hb=hb, wao=wao):
                                if half == 0:
                                    tn.wait_ge(b.S("vrm"), wrm)
                                if wao > 0:
                                    tn.wait_ge(b.S("vao"), wao)
                                ins = None
                                for qt in range(half * 8, half * 8 + 8):
                                    jts2 = ([(qt - 1, 0)] if qt > 0 else []) + [(qt, 128)]
                                    pcol = (qt % 8) * 64
                                    first = True
                                    for (jt, co) in jts2:
                                        tn.matmul(pav[:, pcol:pcol + 64],
                                                  ebuf[:, qt * 256 + co: qt * 256 + co + 128],
                                                  vnat[:, jt * 128 + hb: jt * 128 + hb + 64],
                                                  start=first, stop=False)
                                        first = False
                                    ins = tn.matmul(pav[:, pcol:pcol + 64],
                                                    ec[0:NCK, qt * 128:(qt + 1) * 128],
                                                    vc[0:NCK, hb:hb + 64],
                                                    start=False, stop=True)
                                b.inc(ins, "tav")

                            wav = b.n("tav")

                            @blk.vector
                            def _(v, half=half, hh=hh, hb=hb, wav=wav):
                                v.wait_ge(b.S("tav"), wav)
                                if half == 0 and hh == 0 and wtao_p > 0:
                                    v.wait_ge(b.S("tao"), wtao_p)
                                for qt in range(half * 8, half * 8 + 8):
                                    pcol = (qt % 8) * 64
                                    v.tensor_scalar(aobuf[:, qt * 128 + hb: qt * 128 + hb + 64],
                                                    pav[:, pcol:pcol + 64],
                                                    rmat[:, qt:qt + 1], None,
                                                    op0=ALU.mult)
                                b.inc(v.memset(afl[:], 0.0), "vao")

                    wvo = b.n("vao")
                    for g4 in range(4):
                        wsa = b.n("svt")

                        @blk.tensor
                        def _(tn, g4=g4, wsa=wsa, wvo=wvo):
                            if g4 == 0:
                                tn.wait_ge(b.S("vao"), wvo)
                            if wsa > 0:
                                tn.wait_ge(b.S("svt"), wsa)
                            ins = None
                            for i in range(4):
                                qt = g4 * 4 + i
                                ins = tn.transpose(ptA[:, i * 128:(i + 1) * 128],
                                                   aobuf[:, qt * 128:(qt + 1) * 128], identb[:])
                            b.inc(ins, "tao")

                        wta = b.n("tao")

                        wtpo_p = b.n("tpo")

                        @blk.scalar
                        def _(s, bb=bb, hp=hp, g4=g4, wta=wta, wtpo_p=wtpo_p):
                            s.wait_ge(b.S("tao"), wta)
                            if hp == 0 and g4 == 0 and wtpo_p > 0:
                                s.wait_ge(b.S("tpo"), wtpo_p)
                            s.copy(attnT[:, hp * SQ + g4 * 512: hp * SQ + (g4 + 1) * 512],
                                   ptA[:, 0:512])
                            b.inc(s.copy(afl[:], afl[:]), "svt")

                wat = b.n("svt")
                for mt in range(16):
                    tt = bb * 16 + mt
                    for n2 in range(2):
                        wvr2 = b.n("vaor")

                        @blk.tensor
                        def _(tn, bb=bb, mt=mt, n2=n2, wvr2=wvr2):
                            if mt == 0 and n2 == 0:
                                tn.wait_ge(b.S("svt"), wat)
                                tn.wait_ge(b.S("abl"), b.n("abl"))
                            if wvr2 >= 2:
                                tn.wait_ge(b.S("vaor"), wvr2 - 1)
                            for hp in range(8):
                                tn.matmul(pqk[n2][:],
                                          attnT[:, hp * SQ + mt * 128: hp * SQ + mt * 128 + 128],
                                          woutb[:, hp * E + n2 * 512: hp * E + n2 * 512 + 512],
                                          start=(hp == 0), stop=False)
                            b.inc(tn.matmul(pqk[n2][:], ones1b[:],
                                            boutb[0:1, n2 * 512:(n2 + 1) * 512],
                                            start=False, stop=True), "tpo")

                        wpo = b.n("tpo")

                        @blk.vector
                        def _(v, tt=tt, n2=n2, wpo=wpo):
                            v.wait_ge(b.S("tpo"), wpo)
                            rsl = resid[:, tt * E:(tt + 1) * E]
                            b.inc(v.scalar_tensor_tensor(rsl[:, n2 * 512:(n2 + 1) * 512],
                                                         pqk[n2][:], 1.0,
                                                         rsl[:, n2 * 512:(n2 + 1) * 512],
                                                         op0=ALU.mult, op1=ALU.add), "vaor")

                    resid_ready_attn[tt] = ("vaor", b.n("vaor"))

            if STAGE >= 3:
                attn.close()
                ffnw2 = ExitStack()
                tgr = ffnw2.enter_context(nc.sbuf_tensor("tgr", [128, 8 * INNER], FP8))
                tur = ffnw2.enter_context(nc.sbuf_tensor("tur", [128, 8 * INNER], FP8))
                tdr = ffnw2.enter_context(nc.sbuf_tensor("tdr", [128, 32 * E], FP8))

                AFIN = [(nm, b.n(nm)) for nm in
                        ("vaor", "vao", "svt", "sex", "sec", "vdm", "vqk", "vbm",
                         "vec", "tqk", "tsc", "tcc", "tdn", "tav", "tao", "tvt",
                         "tvc", "tmk", "vmv", "vkc", "tpo", "vmk", "vrm")]

                @blk.sync
                def _(sync):
                    for nm, val in AFIN:
                        if val > 0:
                            sync.wait_ge(b.S(nm), val)
                    for dst, slot in ((tgr, 0), (tur, 1), (tdr, 2)):
                        b.inc(sync.dma_start(out=dst[:], in_=d_tern[slot]), "wl2", 16)

                wl2fin = b.n("wl2")

                @blk.tensor
                def _(tn):
                    for nm, val in AFIN:
                        if val > 0:
                            tn.wait_ge(b.S(nm), val)

                tg, tu, td = tgr, tur, tdr
                f2sc, f2 = open_ffn_scope(2)
                rfin = ("vaor", b.n("vaor"))
                for tt in range(NT):
                    emit_ffn_tile(tt, 2, f2, (b.S(rfin[0]), rfin[1]),
                                  gate_extra_waits=([("wl2", wl2fin)] if tt == 0 else None),
                                  emit_l1_extras=False, phase="front")
                    if tt >= 1:
                        emit_ffn_tile(tt - 1, 2, f2, None, phase="back")
                emit_ffn_tile(NT - 1, 2, f2, None, phase="back")

                @blk.scalar
                def _(s, wfin=b.n("out")):
                    s.wait_ge(b.S("out"), wfin)

                f2sc.close()
                ffnw2.close()

            if STAGE == 2:
                @blk.gpsimd
                def _(g):
                    g.wait_ge(b.S("cbh"), wccb)
                    b.inc(g.dma_start(out=d_dbg, in_=counts[:]), "outg", 16)

                for tt in range(NT):
                    rr = resid_ready_attn[tt]

                    @blk.gpsimd
                    def _(g, tt=tt, rr=rr):
                        g.wait_ge(b.S(rr[0]), rr[1])
                        b.inc(g.dma_start(out=d_out[tt * 128:(tt + 1) * 128, :],
                                          in_=resid[:, tt * E:(tt + 1) * E]), "outg", 16)

                wog = b.n("outg")
                wdbg = b.n("dbg")

                @blk.scalar
                def _(s):
                    s.wait_ge(b.S("outg"), wog)
                    if wdbg:
                        s.wait_ge(b.S("dbg"), wdbg)

            attn.close()


    b.top.close()
    return nc


def kernel(**inputs):
    x = np.ascontiguousarray(np.asarray(inputs["L0_attn_x"], np.float32))
    ident, band01, ded, ones = host_constants()
    nc = build_program()
    common = dict(
        d_wqkv=np.ascontiguousarray(np.asarray(inputs["L0_attn_Wqkv"], np.float32)),
        d_bqkv=np.ascontiguousarray(np.asarray(inputs["L0_attn_bqkv"], np.float32)),
        d_wlif=np.ascontiguousarray(np.asarray(inputs["L0_attn_Wlif"], np.float32)),
        d_blif=np.ascontiguousarray(np.asarray(inputs["L0_attn_blif"], np.float32)),
        d_wout=np.ascontiguousarray(np.asarray(inputs["L0_attn_Wout"], np.float32)),
        d_bout=np.ascontiguousarray(np.asarray(inputs["L0_attn_bout"], np.float32)),
        d_wg=np.ascontiguousarray(np.asarray(inputs["L0_ffn_w_gate"], np.float32)),
        d_wu=np.ascontiguousarray(np.asarray(inputs["L0_ffn_w_up"], np.float32)),
        d_wd=np.ascontiguousarray(np.asarray(inputs["L0_ffn_w_down"], np.float32)),
        d_ident=ident, d_band=band01, d_ded=ded, d_ones=ones,
    )
    in_maps = []
    for c in range(8):
        m = dict(common)
        m["d_x"] = np.ascontiguousarray(x[2 * c:2 * c + 2].reshape(2 * SQ, E))
        in_maps.append(m)
    res = run_bass_kernel_spmd(nc, in_maps, core_ids=list(range(8)))
    out = np.empty((16, SQ, E), np.float32)
    for c in range(8):
        out[2 * c:2 * c + 2] = np.asarray(res.results[c]["d_out"]).reshape(2, SQ, E)
    return out


# version 27 (final)
# speedup vs baseline: 1.0519x; 1.0519x over previous
import numpy as np
from contextlib import ExitStack
import concourse.bass as bass
import concourse.mybir as mybir
from concourse.bass_utils import run_bass_kernel_spmd

F32 = mybir.dt.float32
BF16 = mybir.dt.bfloat16
FP8 = mybir.dt.float8e4
AF = mybir.ActivationFunctionType
ALU = mybir.AluOpType
AX = mybir.AxisListType

MAGIC = 12582912.0
NEG = -3.0e38
E, SQ, H, D, INNER = 1024, 2048, 16, 64, 4096
NT = 32
STAGE = 3
DEBUG = 0
SERIAL_F1 = 0
SERIAL_F2 = 0
CKEYS = [0, 1, 2, 3] + list(range(64, 2048, 64))
NCK = 40


def host_constants():
    ident = np.eye(128, dtype=np.float32)
    p = np.arange(128)[:, None]
    q = np.arange(128)[None, :]
    bandA = ((p >= q) | (p % 64 == 0)).astype(np.float32)
    bandA1 = (((p >= q) | (p % 64 == 0)) | (p < 4)).astype(np.float32)
    bandB = ((p <= q) | (p % 64 == 0)).astype(np.float32)
    bandB0 = (((p <= q) | (p % 64 == 0)) | (p < 4)).astype(np.float32)
    band01 = np.concatenate([bandA, bandA1, bandB, bandB0], axis=1)
    ded = np.zeros((NCK, 16 * 128), np.float32)
    for qt in range(16):
        i0 = qt * 128
        for r, j in enumerate(CKEYS):
            if not (i0 - 128 <= j <= i0 + 127):
                ded[r, qt * 128:(qt + 1) * 128] = 1.0
    ones = np.ones((1, 128), np.float32)
    return ident, band01, ded, ones


class B:
    def __init__(self):
        self.nc = bass.Bass("TRN2", target_bir_lowering=False, debug=False)
        self.top = ExitStack()
        self.sem = {}
        self.cnt = {}

    def S(self, name):
        if name not in self.sem:
            self.sem[name] = self.top.enter_context(self.nc.semaphore(name))
            self.cnt[name] = 0
        return self.sem[name]

    def inc(self, inst, name, k=1):
        inst.then_inc(self.S(name), k)
        self.cnt[name] += k
        return self.cnt[name]

    def n(self, name):
        self.S(name)
        return self.cnt[name]

    def chain_inc(self, eng, inst, tag):
        sem = self.S(tag)
        inst.then_inc(sem, 1)
        self.cnt[tag] += 1
        eng.wait_ge(sem, self.cnt[tag])
        return self.cnt[tag]

    def chain(self, eng, inst, tag):
        sem = self.S(tag)
        inst.then_inc(sem, 1)
        self.cnt[tag] += 1
        eng.wait_ge(sem, self.cnt[tag])
        return inst


def build_program():
    b = B()
    nc = b.nc

    d_x = nc.dram_tensor("d_x", [2 * SQ, E], F32, kind="ExternalInput").ap()
    d_wqkv = nc.dram_tensor("d_wqkv", [E, 3 * E], F32, kind="ExternalInput").ap()
    d_bqkv = nc.dram_tensor("d_bqkv", [3 * E], F32, kind="ExternalInput").ap()
    d_wlif = nc.dram_tensor("d_wlif", [H, D], F32, kind="ExternalInput").ap()
    d_blif = nc.dram_tensor("d_blif", [H], F32, kind="ExternalInput").ap()
    d_wout = nc.dram_tensor("d_wout", [E, E], F32, kind="ExternalInput").ap()
    d_bout = nc.dram_tensor("d_bout", [E], F32, kind="ExternalInput").ap()
    d_wg = nc.dram_tensor("d_wg", [INNER, E], F32, kind="ExternalInput").ap()
    d_wu = nc.dram_tensor("d_wu", [INNER, E], F32, kind="ExternalInput").ap()
    d_wd = nc.dram_tensor("d_wd", [E, INNER], F32, kind="ExternalInput").ap()
    d_ident = nc.dram_tensor("d_ident", [128, 128], F32, kind="ExternalInput").ap()
    d_band = nc.dram_tensor("d_band", [128, 512], F32, kind="ExternalInput").ap()
    d_ded = nc.dram_tensor("d_ded", [NCK, 16 * 128], F32, kind="ExternalInput").ap()
    d_ones = nc.dram_tensor("d_ones", [1, 128], F32, kind="ExternalInput").ap()
    d_out = nc.dram_tensor("d_out", [2 * SQ, E], F32, kind="ExternalOutput").ap()
    d_dbg = (nc.dram_tensor("d_dbg", [H, SQ], F32, kind="ExternalOutput").ap()
             if STAGE < 3 else None)
    d_y2t = nc.dram_tensor("d_y2t", [2, E, SQ], BF16, kind="Internal").ap()
    d_tern = nc.dram_tensor("d_tern", [3, 128, 8 * INNER], FP8, kind="Internal").ap()
    d_ccin = nc.dram_tensor("d_ccin", [H, SQ], F32, kind="Internal").ap()
    d_ccout = nc.dram_tensor("d_ccout", [H, SQ], F32, kind="Internal").ap()
    d_misc = nc.dram_tensor("d_misc", [16, 128], F32, kind="Internal").ap()
    if DEBUG:
        d_dwc = nc.dram_tensor("d_dwc", [128, 8], F32, kind="ExternalOutput").ap()
        d_dtg = nc.dram_tensor("d_dtg", [128, 4096], FP8, kind="ExternalOutput").ap()
        d_dy1 = nc.dram_tensor("d_dy1", [128, E], F32, kind="ExternalOutput").ap()
        d_dxq = nc.dram_tensor("d_dxq", [128, E], BF16, kind="ExternalOutput").ap()
        d_dsg = nc.dram_tensor("d_dsg", [128, 512], F32, kind="ExternalOutput").ap()
        d_dh = nc.dram_tensor("d_dh", [128, INNER], F32, kind="ExternalOutput").ap()
        d_dsc = nc.dram_tensor("d_dsc", [64, 128], BF16, kind="ExternalOutput").ap()
        d_dxqt = nc.dram_tensor("d_dxqt", [128, 1024], BF16, kind="ExternalOutput").ap()
        d_dmr = nc.dram_tensor("d_dmr", [1, 512], F32, kind="ExternalOutput").ap()
        d_dacc = nc.dram_tensor("d_dacc", [128, 36], F32, kind="ExternalOutput").ap()
        d_dtg2 = nc.dram_tensor("d_dtg2", [128, 4096], FP8, kind="ExternalOutput").ap()
        d_dtu = nc.dram_tensor("d_dtu", [128, 8 * 4096], FP8, kind="ExternalOutput").ap()
        d_dtd = nc.dram_tensor("d_dtd", [128, 32 * 1024], FP8, kind="ExternalOutput").ap()
        d_dst35 = nc.dram_tensor("d_dst35", [128, 1024], F32, kind="ExternalOutput").ap()
        d_dtq35 = nc.dram_tensor("d_dtq35", [128, 1024], BF16, kind="ExternalOutput").ap()
        d_dsnap = nc.dram_tensor("d_dsnap", [64, 128, 8], F32, kind="ExternalOutput").ap()

    t_ = b.top.enter_context
    resid = t_(nc.sbuf_tensor("resid", [128, NT * E], BF16))
    identb = t_(nc.sbuf_tensor("identb", [128, 128], BF16))
    ones1b = t_(nc.sbuf_tensor("ones1b", [1, 128], BF16))
    wqlif = t_(nc.sbuf_tensor("wqlif", [128, 8 * 2 * H], BF16))
    blife = t_(nc.sbuf_tensor("blife", [64, 1], F32))
    bqkvc = t_(nc.sbuf_tensor("bqkvc", [128, 32], F32))
    wcols = t_(nc.sbuf_tensor("wcols", [128, 8], F32))
    counts = t_(nc.sbuf_tensor("counts", [H, SQ], BF16))
    scanIc = t_(nc.sbuf_tensor("scanIc", [64, 256], F32))
    cmat = t_(nc.sbuf_tensor("cmat", [64, 128], F32))
    vcol = t_(nc.sbuf_tensor("vcol", [64, 128], F32))
    mwid = t_(nc.sbuf_tensor("mwid", [64, 128], F32))
    colsA = t_(nc.sbuf_tensor("colsA", [128, 32], F32))
    itmp = t_(nc.sbuf_tensor("itmp", [128, 32], F32))
    escr = t_(nc.sbuf_tensor("escr", [64, 128], BF16))
    escr2 = t_(nc.sbuf_tensor("escr2", [16, 128], BF16))

    def emit_scan_chunk(jc):
        if DEBUG and jc == 0:
            @blk.vector
            def _(v):
                b.inc(v.tensor_copy(escr[:], scanIc[0:64, 0:128]), "dsc")

            wdsc = b.n("dsc")

            @blk.sync
            def _(sync):
                sync.wait_ge(b.S("dsc"), wdsc)
                b.inc(sync.dma_start(out=d_dsc, in_=escr[:]), "dbg", 16)
        for part in range(4):
            emit_scan_part(jc, part)

    def emit_scan_part(jc, part):
        @blk.vector
        def _(v, jc=jc, part=part):
            base = (jc % 2) * 128
            ins = None
            for i in range(part * 32, part * 32 + 32):
                icol = scanIc[:, base + i:base + i + 1]
                if jc == 0 and i == 0:
                    v.memset(vcol[:], 0.0)
                    v.tensor_scalar(vcol[:], vcol[:], 1.0, icol,
                                    op0=ALU.mult, op1=ALU.add)
                else:
                    v.tensor_scalar(vcol[:], vcol[:], mwid[:, 0:1], icol,
                                    op0=ALU.mult, op1=ALU.add)
                v.tensor_copy(cmat[:, i:i + 1], vcol[:, 0:1])
                ins = v.tensor_scalar(mwid[:], vcol[:], 1.0, 0.9,
                                      op0=ALU.is_lt, op1=ALU.mult)
            if part == 3:
                b.chain(v, ins, "vch")
                b.chain(v, v.tensor_scalar(escr[:], cmat[:], 1.0, 1.0,
                                           op0=ALU.is_lt, op1=ALU.is_lt), "vch")
                b.chain(v, v.tensor_copy(escr2[:], escr[32:48, :]), "vch")
                v.tensor_add(counts[:, jc * 128:(jc + 1) * 128],
                             escr[0:16, :], escr2[:])
                b.inc(v.memset(cmat[:, 0:1], 0.0), "vcnt")

    with nc.Block() as blk:
        inita = ExitStack()
        wqq = inita.enter_context(nc.sbuf_tensor("wqq", [128, 8 * E], F32, side="right"))
        idf = inita.enter_context(nc.sbuf_tensor("idf", [128, 128], F32, side="right"))
        hdt = inita.enter_context(nc.sbuf_tensor("hdt", [16, 64], F32, side="right"))
        wlf = inita.enter_context(nc.sbuf_tensor("wlf", [16, 64], F32, side="right"))
        wlbc = inita.enter_context(nc.sbuf_tensor("wlbc", [128, E], F32, side="right"))
        wlfr = inita.enter_context(nc.sbuf_tensor("wlfr", [1, E], F32, side="right"))
        wlfh = inita.enter_context(nc.sbuf_tensor("wlfh", [1, E], BF16, side="right"))
        wlfl = inita.enter_context(nc.sbuf_tensor("wlfl", [1, E], BF16, side="right"))
        wlft = inita.enter_context(nc.sbuf_tensor("wlft", [1, E], F32, side="right"))
        wqlf = inita.enter_context(nc.sbuf_tensor("wqlf", [128, 8 * H], F32, side="right"))
        bqn = inita.enter_context(nc.sbuf_tensor("bqn", [32, 128], F32, side="right"))
        rows = inita.enter_context(nc.sbuf_tensor("rows", [1, 128], F32, side="right"))
        tsc = inita.enter_context(nc.sbuf_tensor("tsc", [32, 32], F32, side="right"))
        tsc2 = inita.enter_context(nc.sbuf_tensor("tsc2", [32, 32], F32, side="right"))
        prodw = inita.enter_context(nc.sbuf_tensor("prodw", [128, E], F32, side="right"))
        ipb = inita.enter_context(nc.psum_tensor("ipb", [128, 512], F32, side="right"))

        @blk.vector
        def _(v):
            v.memset(tsc[:], 0.0)
            v.memset(tsc2[:], 0.0)
            v.memset(bqn[:], 0.0)
            v.memset(blife[:], 0.0)
            v.memset(vcol[:], 0.0)
            v.memset(itmp[:], 0.0)
            v.memset(cmat[:], 0.0)
            v.memset(counts[:], 0.0)
            v.memset(wcols[:], 0.0)
            b.inc(v.memset(scanIc[:], 0.0), "vz")

        wz = b.n("vz")

        @blk.sync
        def _(sync):
            sync.wait_ge(b.S("vz"), wz)
            b.inc(sync.dma_start(out=idf[:], in_=d_ident), "ld", 16)
            b.inc(sync.dma_start(out=hdt[:], in_=d_bqkv[0:E].rearrange("(h d) -> h d", d=64)),
                  "ld", 16)
            b.inc(sync.dma_start(out=wlf[:], in_=d_wlif), "ld", 16)
            b.inc(sync.dma_start(out=tsc[0:1, 0:16], in_=d_blif.rearrange("(o h) -> o h", o=1)), "ld", 16)
            b.inc(sync.dma_start(out=rows[:], in_=d_ones), "ld", 16)
            b.inc(sync.dma_start(out=bqn[0:24, :], in_=d_bqkv.rearrange("(t p) -> t p", p=128)),
                  "ld", 16)

        @blk.gpsimd
        def _(g):
            g.wait_ge(b.S("vz"), wz)
            b.inc(g.dma_start(out=wlfr[:], in_=d_wlif.rearrange("(o h) d -> o (h d)", o=1)), "ldg", 16)
            b.inc(g.dma_start(out=wqq[:].rearrange("p (kt n) -> p kt n", n=E),
                              in_=d_wqkv[:, 0:E].rearrange("(kt p) n -> p kt n", p=128)),
                  "ldg", 16)

            b.inc(g.dma_start(out=resid[:].rearrange("p (tt d) -> p tt d", d=E),
                              in_=d_x.rearrange("(tt p) d -> p tt d", p=128)),
                  "x", 16)

        w_ld = b.n("ld")

        @blk.vector
        def _(v):
            v.wait_ge(b.S("ld"), w_ld)
            v.wait_ge(b.S("ldg"), b.n("ldg"))
            v.tensor_copy(identb[:], idf[:])
            v.tensor_copy(ones1b[:], rows[0:1, 0:128])
            b.chain(v, v.transpose(tsc2[:], tsc[:]), "vch")
            b.chain(v, v.tensor_mul(hdt[:], hdt[:], wlf[:]), "vch")
            b.chain(v, v.reduce_sum(itmp[0:16, 0:1], hdt[:], axis=AX.X), "vch")
            b.chain(v, v.tensor_add(blife[0:16, :], tsc2[0:16, 0:1], itmp[0:16, 0:1]), "vch")
            v.tensor_copy(blife[32:48, :], blife[0:16, :])
            for c in range(4):
                v.transpose(bqkvc[c * 32:(c + 1) * 32, 0:32], bqn[0:32, c * 32:(c + 1) * 32])
            b.inc(v.memset(itmp[:], 0.0), "vp")

        @blk.vector
        def _(v):
            v.tensor_copy(wlfh[:], wlfr[:])
            b.chain(v, v.tensor_copy(wlft[:], wlfh[:]), "vch")
            v.tensor_sub(wlft[:], wlfr[:], wlft[:])
            b.inc(v.tensor_copy(wlfl[:], wlft[:]), "vp")

        for half in range(2):
            wv = b.n("vp")
            wvwb = b.n("vwb")

            @blk.tensor
            def _(tn, half=half, wv=wv, wvwb=wvwb):
                tn.wait_ge(b.S("vp"), wv)
                if half == 1:
                    tn.wait_ge(b.S("vwb"), wvwb)
                tn.matmul(ipb[:], ones1b[:], wlfh[0:1, half * 512:(half + 1) * 512],
                          start=True, stop=False)
                b.inc(tn.matmul(ipb[:], ones1b[:], wlfl[0:1, half * 512:(half + 1) * 512],
                                start=False, stop=True), "twb")

            wt = b.n("twb")

            @blk.vector
            def _(v, half=half, wt=wt):
                v.wait_ge(b.S("twb"), wt)
                b.inc(v.tensor_copy(wlbc[:, half * 512:(half + 1) * 512], ipb[:]), "vwb")

        wv = b.n("vwb")

        @blk.vector
        def _(v):
            v.wait_ge(b.S("vwb"), wv)
            for kt in range(8):
                v.tensor_mul(prodw[:], wqq[:, kt * E:(kt + 1) * E], wlbc[:])
                ins = None
                for h in range(H):
                    ins = v.reduce_sum(wqlf[:, kt * H + h:kt * H + h + 1],
                                       prodw[:, h * 64:(h + 1) * 64], axis=AX.X)
                b.chain(v, ins, "vch")
            wq2 = wqlif[:].rearrange("p (kt t h) -> p kt t h", t=2, h=H)
            b.chain(v, v.tensor_copy(wq2[:, :, 0, :], wqlf[:].rearrange("p (kt h) -> p kt h", h=H)), "vch")
            b.chain(v, v.tensor_copy(prodw[:, 0:8 * H], wq2[:, :, 0, :]), "vch")
            b.chain(v, v.tensor_sub(prodw[:, 0:8 * H], wqlf[:], prodw[:, 0:8 * H]), "vch")
            ins = v.tensor_copy(wq2[:, :, 1, :], prodw[:, 0:8 * H].rearrange("p (kt h) -> p kt h", h=H))
            b.chain(v, ins, "vch")
            b.inc(v.memset(itmp[:, 31:32], 0.0), "vp")
        winita = b.n("vp")
        inita.close()

        initb = ExitStack()
        stage = initb.enter_context(nc.sbuf_tensor("stage", [128, 2 * 2048], F32, side="right"))
        tquant = initb.enter_context(nc.sbuf_tensor("tquant", [128, 2048], BF16, side="right"))
        stq = initb.enter_context(nc.sbuf_tensor("stq", [128, 2048], F32, side="right"))
        acc = initb.enter_context(nc.sbuf_tensor("acc", [128, 36], F32, side="right"))
        mrow1 = initb.enter_context(nc.sbuf_tensor("mrow1", [1, 512], F32, side="right"))
        hrow = initb.enter_context(nc.sbuf_tensor("hrow", [1, 12], BF16, side="right"))
        ipt = initb.enter_context(nc.psum_tensor("ipt", [128, 1024], BF16, side="right"))
        ipb2 = initb.enter_context(nc.psum_tensor("ipb2", [128, 512], F32, side="right"))

        ffnw = ExitStack()
        tg = ffnw.enter_context(nc.sbuf_tensor("tg", [128, 8 * INNER], FP8))
        tu = ffnw.enter_context(nc.sbuf_tensor("tu", [128, 8 * INNER], FP8))
        td = ffnw.enter_context(nc.sbuf_tensor("td", [128, 32 * E], FP8))

        WSPEC = [(d_wg, 32, E, tg, INNER, 0), (d_wu, 32, E, tu, INNER, 1),
                 (d_wd, 8, INNER, td, E, 2)]

        @blk.vector
        def _(v):
            b.inc(v.memset(acc[:], 0.0), "vacz")

        wacz = b.n("vacz")

        def w_pieces(wid):
            return [(o, min(2048, wid - o)) for o in range(0, wid, 2048)]

        for wi, (dw, ntl, wid, dst, dcols, slot) in enumerate(WSPEC):
            for ot in range(ntl):
                for (po, pw) in w_pieces(wid):
                    par = (b.n("wld") // 16) % 2

                    first_wld = (b.n("wld") == 0)

                    @blk.sync
                    def _(sync, dw=dw, ot=ot, po=po, pw=pw, par=par, first_wld=first_wld):
                        if first_wld:
                            sync.wait_ge(b.S("vp"), winita)
                        if b.n("vabs") >= 2:
                            sync.wait_ge(b.S("vabs"), b.n("vabs") - 1)
                        b.inc(sync.dma_start(out=stage[:, par * 2048: par * 2048 + pw],
                                             in_=dw[ot * 128:(ot + 1) * 128, po:po + pw]),
                              "wld", 16)

                    wldn = b.n("wld")

                    @blk.vector
                    def _(v, wi=wi, pw=pw, par=par, wldn=wldn):
                        v.wait_ge(b.S("wld"), wldn)
                        if b.n("vabs") == 0:
                            v.wait_ge(b.S("vacz"), wacz)
                        b.chain(v, v.reduce_sum(acc[:, 35:36],
                                                stage[:, par * 2048: par * 2048 + pw],
                                                axis=AX.X, apply_absolute_value=True), "vch")
                        ins = v.tensor_add(acc[:, wi:wi + 1], acc[:, wi:wi + 1], acc[:, 35:36])
                        b.chain_inc(v, ins, "vabs")

        wabs = b.n("vabs")

        @blk.vector
        def _(v):
            v.wait_ge(b.S("vabs"), wabs)
            for wi in range(3):
                ins = None
                for c in range(4):
                    ins = v.transpose(cmat[0:32, c * 32:(c + 1) * 32],
                                      acc[c * 32:(c + 1) * 32, wi:wi + 32])
                b.chain(v, ins, "vch")
                b.chain(v, v.reduce_sum(mrow1[0:1, 384 + wi:385 + wi],
                                        cmat[0:1, 0:128], axis=AX.X), "vch")
            b.chain(v, v.tensor_scalar(mrow1[0:1, 390:393], mrow1[0:1, 384:387],
                                       1.0 / (INNER * E), 1e-5, op0=ALU.mult, op1=ALU.max), "vch")
            b.chain(v, v.reciprocal(mrow1[0:1, 393:396], mrow1[0:1, 390:393]), "vch")
            b.chain(v, v.reciprocal(mrow1[0:1, 396:399], mrow1[0:1, 393:396]), "vch")
            b.chain(v, v.tensor_copy(hrow[0:1, 0:3], mrow1[0:1, 393:396]), "vch")
            b.chain(v, v.tensor_copy(hrow[0:1, 6:9], mrow1[0:1, 396:399]), "vch")
            b.chain(v, v.tensor_copy(mrow1[0:1, 399:402], hrow[0:1, 0:3]), "vch")
            b.chain(v, v.tensor_copy(mrow1[0:1, 402:405], hrow[0:1, 6:9]), "vch")
            b.chain(v, v.tensor_sub(mrow1[0:1, 405:408], mrow1[0:1, 393:396],
                                    mrow1[0:1, 399:402]), "vch")
            b.chain(v, v.tensor_sub(mrow1[0:1, 408:411], mrow1[0:1, 396:399],
                                    mrow1[0:1, 402:405]), "vch")
            b.chain(v, v.tensor_copy(hrow[0:1, 3:6], mrow1[0:1, 405:408]), "vch")
            ins = v.tensor_copy(hrow[0:1, 9:12], mrow1[0:1, 408:411])
            b.chain(v, ins, "vch")
            b.inc(v.memset(acc[:, 34:35], 0.0), "vp")

        for wi in range(3):
            for which in range(2):
                wv = b.n("vp")
                wbcv = b.n("vbc")

                @blk.tensor
                def _(tn, wi=wi, which=which, wv=wv, wbcv=wbcv):
                    tn.wait_ge(b.S("vp"), wv)
                    if wbcv > 0:
                        tn.wait_ge(b.S("vbc"), wbcv)
                    chi = which * 6 + wi
                    tn.matmul(ipb2[:, 0:1], ones1b[:], hrow[0:1, chi:chi + 1],
                              start=True, stop=False)
                    b.inc(tn.matmul(ipb2[:, 0:1], ones1b[:], hrow[0:1, chi + 3:chi + 4],
                                    start=False, stop=True), "tbc")

                wt = b.n("tbc")

                @blk.vector
                def _(v, wi=wi, which=which, wt=wt):
                    v.wait_ge(b.S("tbc"), wt)
                    b.inc(v.tensor_copy(wcols[:, 3 * which + wi:3 * which + wi + 1],
                                        ipb2[:, 0:1]), "vbc")

        for wi, (dw, ntl, wid, dst, dcols, slot) in enumerate(WSPEC):
            for ot in range(ntl):
                for (po, pw) in w_pieces(wid):
                    par = (b.n("wld") // 16) % 2

                    @blk.sync
                    def _(sync, dw=dw, ot=ot, po=po, pw=pw, par=par):
                        if b.n("vq2") >= 2:
                            sync.wait_ge(b.S("vq2"), b.n("vq2") - 1)
                        else:
                            sync.wait_ge(b.S("vabs"), wabs)
                        b.inc(sync.dma_start(out=stage[:, par * 2048: par * 2048 + pw],
                                             in_=dw[ot * 128:(ot + 1) * 128, po:po + pw]),
                              "wld", 16)

                    wldn = b.n("wld")
                    wt8 = b.n("t8")

                    @blk.vector
                    def _(v, wi=wi, pw=pw, par=par, wldn=wldn, wt8=wt8):
                        v.wait_ge(b.S("wld"), wldn)
                        if wt8 > 0:
                            v.wait_ge(b.S("t8"), wt8)
                        st = stage[:, par * 2048: par * 2048 + pw]
                        sq = stq[:, 0:pw]
                        v.tensor_scalar(sq, st, wcols[:, wi:wi + 1], MAGIC,
                                        op0=ALU.mult, op1=ALU.add)
                        v.tensor_scalar(sq, sq, MAGIC, None, op0=ALU.subtract)
                        v.tensor_scalar(tquant[:, 0:pw], sq, -1.0, 1.0,
                                        op0=ALU.max, op1=ALU.min)
                        b.inc(v.memset(stq[:, 0:1], 0.0), "vq2")

                    nkb = pw // 128
                    for g0 in range(0, nkb, 4):
                        gn = min(4, nkb - g0)
                        wq2 = b.n("vq2")
                        ws8 = b.n("se8")

                        wbcf = b.n("vbc")

                        @blk.tensor
                        def _(tn, g0=g0, gn=gn, wq2=wq2, ws8=ws8, wbcf=wbcf):
                            tn.wait_ge(b.S("vq2"), wq2)
                            if ws8 > 0:
                                tn.wait_ge(b.S("se8"), ws8)
                            else:
                                tn.wait_ge(b.S("vbc"), wbcf)
                            ins = None
                            for i in range(gn):
                                ins = tn.transpose(ipt[:, i * 128:(i + 1) * 128],
                                                   tquant[:, (g0 + i) * 128:(g0 + i + 1) * 128],
                                                   identb[:])
                            b.inc(ins, "t8")

                        wt8b = b.n("t8")

                        @blk.scalar
                        def _(s, wi=wi, ot=ot, po=po, g0=g0, gn=gn, dst=dst, dcols=dcols,
                              wt8b=wt8b):
                            s.wait_ge(b.S("t8"), wt8b)
                            kb0 = po // 128 + g0
                            dv = dst[:].rearrange("p (k c) -> p k c", c=dcols)[
                                :, kb0:kb0 + gn, ot * 128:(ot + 1) * 128]
                            sv = ipt[:, 0:gn * 128].rearrange("p (g c) -> p g c", c=128)
                            b.inc(s.copy(dv, sv), "se8")

            wse = b.n("se8")

            @blk.scalar
            def _(s, dst=dst, slot=slot, wse=wse):
                s.wait_ge(b.S("se8"), wse)
                b.inc(s.dma_start(out=d_tern[slot], in_=dst[:]), "tsp", 16)

        if DEBUG:
            wv_d = b.n("vbc")
            wse_d = b.n("se8")

            @blk.sync
            def _(sync):
                sync.wait_ge(b.S("vbc"), wv_d)
                sync.wait_ge(b.S("se8"), wse_d)
                b.inc(sync.dma_start(out=d_dwc, in_=wcols[:]), "dbg", 16)
                b.inc(sync.dma_start(out=d_dtg, in_=tg[:, 0:4096]), "dbg", 16)
                b.inc(sync.dma_start(out=d_dtg2, in_=tg[:, 3 * 4096:4 * 4096]), "dbg", 16)
                b.inc(sync.dma_start(out=d_dtu, in_=tu[:]), "dbg", 16)
                b.inc(sync.dma_start(out=d_dtd, in_=td[:]), "dbg", 16)
                b.inc(sync.dma_start(out=d_dmr, in_=mrow1[:]), "dbg", 16)
                b.inc(sync.dma_start(out=d_dacc, in_=acc[:]), "dbg", 16)

        initb.close()

        @blk.scalar
        def _(s, wq2f=b.n("vq2")):
            s.wait_ge(b.S("vq2"), wq2f)

        @blk.tensor
        def _(tn, ws8f=b.n("se8"), wbcf=b.n("vbc")):
            tn.wait_ge(b.S("se8"), ws8f)
            tn.wait_ge(b.S("vbc"), wbcf)

        def open_ffn_scope(tagnum):
            sc = ExitStack()
            f = {}
            f["y1"] = sc.enter_context(nc.sbuf_tensor(f"y1_{tagnum}", [128, E], F32, side="right"))
            f["xq"] = sc.enter_context(nc.sbuf_tensor(f"xq_{tagnum}", [128, E], BF16, side="right"))
            f["xqT"] = sc.enter_context(nc.sbuf_tensor(f"xqT_{tagnum}", [128, 8 * 128], BF16, side="right"))
            nhb = 2 if tagnum == 2 else 1
            f["hbuf"] = [sc.enter_context(nc.sbuf_tensor(f"hbuf_{tagnum}_{i}", [128, INNER], BF16, side="right"))
                         for i in range(nhb)]
            f["sg"] = sc.enter_context(nc.sbuf_tensor(f"sg_{tagnum}", [128, 2 * 512],
                                                      BF16 if tagnum == 2 else F32, side="right"))
            f["qhb"] = sc.enter_context(nc.sbuf_tensor(
                f"qhb_{tagnum}", [128, 1024 if tagnum == 2 else 512], BF16, side="right"))
            f["qhT"] = sc.enter_context(nc.sbuf_tensor(f"qhT_{tagnum}", [128, 32 * 128], BF16, side="right"))
            if tagnum == 1:
                f["y2tt"] = sc.enter_context(nc.sbuf_tensor("y2tt", [128, 8 * 128], BF16, side="right"))
            else:
                f["outt"] = sc.enter_context(nc.sbuf_tensor("outt", [128, E], BF16, side="right"))
            f["pg"] = [sc.enter_context(nc.psum_tensor(f"pg{i}_{tagnum}", [128, 512], F32, side="right"))
                       for i in range(2)]
            f["pu"] = [sc.enter_context(nc.psum_tensor(f"pu{i}_{tagnum}", [128, 512], F32, side="right"))
                       for i in range(2)]
            f["pd"] = [sc.enter_context(nc.psum_tensor(f"pd{i}_{tagnum}", [128, 512], F32, side="right"))
                       for i in range(2)]
            f["pt"] = sc.enter_context(nc.psum_tensor(f"pt_{tagnum}", [128, 1024], BF16, side="right"))
            f["pi"] = sc.enter_context(
                nc.psum_tensor(f"pi_{tagnum}", [128, 1024], BF16, side="right")
                if tagnum == 2 else
                nc.psum_tensor(f"pi_{tagnum}", [128, 512], F32, side="right"))
            return sc, f

        def emit_ln(rsl, ca, L, wr, y1, yfree):

            @blk.vector
            def _(v):
                if wr is not None:
                    v.wait_ge(wr[0], wr[1])
                b.chain_inc(v, v.reduce_sum(ca[:, 0:1], rsl, axis=AX.X), "vls" + L)

            wy1free = b.n(yfree)

            @blk.scalar
            def _(s):
                if wr is not None:
                    s.wait_ge(wr[0], wr[1])
                if wy1free > 0:
                    s.wait_ge(b.S(yfree), wy1free)
                b.inc(s.activation(y1[:], rsl, AF.Square), "sls" + L)

            wss = b.n("sls" + L)

            @blk.vector
            def _(v):
                v.wait_ge(b.S("sls" + L), wss)
                b.chain(v, v.reduce_sum(ca[:, 1:2], y1[:], axis=AX.X), "vch")
                b.chain(v, v.tensor_scalar(ca[:, 2:3], ca[:, 0:1], 1.0 / E, None,
                                           op0=ALU.mult), "vch")
                b.chain(v, v.tensor_mul(ca[:, 3:4], ca[:, 2:3], ca[:, 2:3]), "vch")
                b.chain(v, v.scalar_tensor_tensor(ca[:, 4:5], ca[:, 1:2], 1.0 / E,
                                                  ca[:, 3:4], op0=ALU.mult,
                                                  op1=ALU.subtract), "vch")
                ins = v.tensor_scalar(ca[:, 5:6], ca[:, 4:5], 1e-5, None, op0=ALU.add)
                b.chain_inc(v, ins, "vve" + L)

            wve = b.n("vve" + L)

            @blk.scalar
            def _(s):
                s.wait_ge(b.S("vve" + L), wve)
                b.chain(s, s.activation(ca[:, 6:7], ca[:, 5:6], AF.Ln), "sch")
                b.inc(s.activation(ca[:, 7:8], ca[:, 6:7], AF.Exp, scale=-0.5), "srs" + L)

            wrs = b.n("srs" + L)

            @blk.vector
            def _(v):
                v.wait_ge(b.S("srs" + L), wrs)
                b.inc(v.tensor_scalar(y1[:], rsl, ca[:, 2:3], ca[:, 7:8],
                                      op0=ALU.subtract, op1=ALU.mult), "vy1" + L)

        dbg_guard = {0: 0, 1: 0}
        pv = {1: {0: {}, 1: {}}, 2: {0: {}, 1: {}}}

        def emit_ffn_tile(tt, layer, f, resid_ready, gate_extra_waits=None,
                          emit_l1_extras=True, eo=None, phase=None):
            L = "L%d" % layer
            if (SERIAL_F1 if layer == 1 else SERIAL_F2):
                wb = b.n("bar")

                @blk.vector
                def _(v):
                    v.wait_ge(b.S("bar"), wb)

                @blk.scalar
                def _(s):
                    s.wait_ge(b.S("bar"), wb)

                @blk.tensor
                def _(tn):
                    tn.wait_ge(b.S("bar"), wb)
            tgn = str(layer)
            pp = (tt if eo is None else eo) % 2
            ca = colsA[:, 16 * pp:16 * (pp + 1)]
            rsl = resid[:, tt * E:(tt + 1) * E]
            y1, xq, xqT = f["y1"], f["xq"], f["xqT"]
            hbuf = f["hbuf"][pp % len(f["hbuf"])]
            sg, qhb, qhT = f["sg"], f["qhb"], f["qhT"]
            pg, pu, pd, pt, pi = f["pg"], f["pu"], f["pd"], f["pt"], f["pi"]

            if phase == "back":
                return emit_ffn_back(tt, layer, f, eo)
            yfree_sem = "vy2b" if layer == 1 else ("vxq" + L)
            emit_ln(rsl, ca, L, resid_ready, y1, yfree_sem)

            wtx = b.n("txq" + L)
            wty2 = b.n("ty2")

            wg_pp = dbg_guard[pp]

            @blk.vector
            def _(v):
                if DEBUG and layer == 1 and wg_pp > 0:
                    v.wait_ge(b.S("dbg"), wg_pp)
                b.chain(v, v.reduce_max(ca[:, 8:9], y1[:], axis=AX.X,
                                        apply_absolute_value=True), "vch")
                b.chain(v, v.tensor_scalar(ca[:, 9:10], ca[:, 8:9], 1e-5, None,
                                           op0=ALU.max), "vch")
                b.chain(v, v.tensor_scalar(ca[:, 10:11], ca[:, 9:10], 1.0 / 127.0, None,
                                           op0=ALU.mult), "vch")
                b.chain(v, v.reciprocal(ca[:, 11:12], ca[:, 9:10]), "vch")
                b.chain(v, v.tensor_scalar(ca[:, 11:12], ca[:, 11:12], 127.0, None,
                                           op0=ALU.mult), "vch")
                b.chain(v, v.tensor_mul(ca[:, 12:13], ca[:, 10:11], wcols[:, 3:4]), "vch")
                b.chain(v, v.tensor_mul(ca[:, 13:14], ca[:, 10:11], wcols[:, 4:5]), "vch")
                if wtx > 0:
                    v.wait_ge(b.S("txq" + L), wtx)
                if layer == 1 and wty2 > 0:
                    v.wait_ge(b.S("ty2"), wty2)
                v.tensor_scalar(y1[:], y1[:], ca[:, 11:12], MAGIC, op0=ALU.mult, op1=ALU.add)
                v.tensor_scalar(xq[:], y1[:], MAGIC, None, op0=ALU.subtract)
                b.inc(v.memset(ca[:, 15:16], 0.0), "vxq" + L)

            wxq = b.n("vxq" + L)
            if DEBUG and layer == 1:
                @blk.sync
                def _(sync):
                    sync.wait_ge(b.S("vxq" + L), wxq)
                    b.inc(sync.dma_start(out=d_dsnap[2 * tt], in_=ca[:, 8:16]), "dbg", 16)
            if DEBUG and tt == 0 and layer == 1:
                wy1d = b.n("vy1" + L)

                @blk.sync
                def _(sync):
                    sync.wait_ge(b.S("vy1" + L), wy1d)
                    b.inc(sync.dma_start(out=d_dy1, in_=y1[:]), "dbg", 16)

                @blk.sync
                def _(sync):
                    sync.wait_ge(b.S("vxq" + L), wxq)
                    b.inc(sync.dma_start(out=d_dxq, in_=xq[:]), "dbg", 16)

            wfr = b.n("ptf" + L)

            @blk.tensor
            def _(tn, wfr=wfr):
                tn.wait_ge(b.S("vxq" + L), wxq)
                if wfr > 0:
                    tn.wait_ge(b.S("ptf" + L), wfr)
                ins = None
                for i in range(8):
                    ins = tn.transpose(pt[:, i * 128:(i + 1) * 128],
                                       xq[:, i * 128:(i + 1) * 128], identb[:])
                b.inc(ins, "txq" + L)

            wtx2 = b.n("txq" + L)
            wgu_prev = b.n("tgu" + L)

            @blk.scalar
            def _(s, wtx2=wtx2, wgu_prev=wgu_prev):
                s.wait_ge(b.S("txq" + L), wtx2)
                if wgu_prev > 0:
                    s.wait_ge(b.S("tgu" + L), wgu_prev)
                s.copy(xqT[:], pt[:, 0:1024])
                b.inc(s.copy(colsA[:, 30:31], colsA[:, 30:31]), "ptf" + L)

            wxt = b.n("ptf" + L)
            if DEBUG and tt == 0 and layer == 1:
                @blk.sync
                def _(sync):
                    sync.wait_ge(b.S("ptf" + L), wxt)
                    b.inc(sync.dma_start(out=d_dxqt, in_=xqT[:]), "dbg", 16)

            for ncn in range(8):
                par = ncn % 2
                wfg = b.n("ssg" + L)
                wfu = b.n("vh" + L)

                @blk.tensor
                def _(tn, ncn=ncn, par=par, wfg=wfg, wfu=wfu):
                    if ncn == 0:
                        tn.wait_ge(b.S("ptf" + L), wxt)
                        if gate_extra_waits:
                            for snm, sval in gate_extra_waits:
                                if sval > 0:
                                    tn.wait_ge(b.S(snm), sval)
                    if wfg >= 2:
                        tn.wait_ge(b.S("ssg" + L), wfg - 1)
                    if wfu >= 2:
                        tn.wait_ge(b.S("vh" + L), wfu - 1)
                    for k in range(8):
                        tn.matmul(pg[par][:], xqT[:, k * 128:(k + 1) * 128],
                                  tg[:, k * INNER + ncn * 512: k * INNER + ncn * 512 + 512],
                                  start=(k == 0), stop=(k == 7))
                    ins = None
                    for k in range(8):
                        ins = tn.matmul(pu[par][:], xqT[:, k * 128:(k + 1) * 128],
                                        tu[:, k * INNER + ncn * 512: k * INNER + ncn * 512 + 512],
                                        start=(k == 0), stop=(k == 7))
                    b.inc(ins, "tgu" + L)

                wgu = b.n("tgu" + L)
                wsgfree = b.n("vh" + L)

                @blk.scalar
                def _(s, ncn=ncn, par=par, wgu=wgu, wsgfree=wsgfree):
                    s.wait_ge(b.S("tgu" + L), wgu)
                    if ncn == 0:
                        s.wait_ge(b.S("vxq" + L), wxq)
                    if wsgfree >= 2:
                        s.wait_ge(b.S("vh" + L), wsgfree - 1)
                    b.inc(s.activation(sg[:, par * 512:(par + 1) * 512], pg[par][:],
                                       AF.Silu, scale=ca[:, 12:13]), "ssg" + L)

                wsg = b.n("ssg" + L)

                @blk.vector
                def _(v, ncn=ncn, par=par, wsg=wsg):
                    v.wait_ge(b.S("ssg" + L), wsg)
                    b.inc(v.scalar_tensor_tensor(hbuf[:, ncn * 512:(ncn + 1) * 512],
                                                 pu[par][:], ca[:, 13:14],
                                                 sg[:, par * 512:(par + 1) * 512],
                                                 op0=ALU.mult, op1=ALU.mult), "vh" + L)

            if DEBUG and tt == 0 and layer == 1:
                wsg_d = b.n("ssg" + L)
                wh_d = b.n("vh" + L)

                @blk.sync
                def _(sync):
                    sync.wait_ge(b.S("ssg" + L), wsg_d - 7)
                    b.inc(sync.dma_start(out=d_dsg, in_=sg[:, 0:512]), "dbg", 16)

                @blk.sync
                def _(sync):
                    sync.wait_ge(b.S("vh" + L), wh_d)
                    b.inc(sync.dma_start(out=d_dh, in_=hbuf[:]), "dbg", 16)

            if phase == "front":
                return

            emit_ffn_back(tt, layer, f, eo)

        def emit_ffn_back(tt, layer, f, eo):
            L = "L%d" % layer
            pp = (tt if eo is None else eo) % 2
            ca = colsA[:, 16 * pp:16 * (pp + 1)]
            rsl = resid[:, tt * E:(tt + 1) * E]
            y1, xq, xqT = f["y1"], f["xq"], f["xqT"]
            hbuf = f["hbuf"][pp % len(f["hbuf"])]
            sg, qhb, qhT = f["sg"], f["qhb"], f["qhT"]
            pg, pu, pd, pt, pi = f["pg"], f["pu"], f["pd"], f["pt"], f["pi"]
            yfree_sem = "vy2b" if layer == 1 else ("vxq" + L)
            emit_l1_extras = (layer == 1)

            wg_xs = b.n("dbg")

            @blk.vector
            def _(v):
                if DEBUG and layer == 1 and wg_xs > 0:
                    v.wait_ge(b.S("dbg"), wg_xs)
                b.chain(v, v.reduce_max(ca[:, 8:9], hbuf[:], axis=AX.X,
                                        apply_absolute_value=True), "vch")
                b.chain(v, v.tensor_scalar(ca[:, 9:10], ca[:, 8:9], 1e-5, None,
                                           op0=ALU.max), "vch")
                b.chain(v, v.tensor_scalar(ca[:, 10:11], ca[:, 9:10], 1.0 / 127.0, None,
                                           op0=ALU.mult), "vch")
                b.chain(v, v.reciprocal(ca[:, 11:12], ca[:, 9:10]), "vch")
                b.chain(v, v.tensor_scalar(ca[:, 11:12], ca[:, 11:12], 127.0, None,
                                           op0=ALU.mult), "vch")
                b.chain(v, v.tensor_mul(ca[:, 14:15], ca[:, 10:11], wcols[:, 5:6]), "vch")
                b.chain(v, v.tensor_scalar(ca[:, 14:15], ca[:, 14:15], 0.5, None,
                                           op0=ALU.mult), "vch")


            if DEBUG and layer == 1:
                wq0 = b.n("vqh" + L) + 1

                @blk.sync
                def _(sync, wq0=wq0):
                    sync.wait_ge(b.S("vqh" + L), wq0)
                    b.inc(sync.dma_start(out=d_dsnap[2 * tt + 1], in_=ca[:, 8:16]), "dbg", 16)
                dbg_guard[pp] = b.n("dbg")

            for ncn in range(8):
                wqf = b.n("tqh" + L)

                qs = (qhb[:, (ncn % 2) * 512:(ncn % 2 + 1) * 512] if layer == 2
                      else qhb[:, 0:512])
                ptb = (pi if (layer == 2 and ncn % 2 == 0) else pt)

                @blk.vector
                def _(v, ncn=ncn, wqf=wqf, qs=qs):
                    wqf2 = wqf - 1 if layer == 2 else wqf
                    if wqf2 > 0:
                        v.wait_ge(b.S("tqh" + L), wqf2)
                    hs = hbuf[:, ncn * 512:(ncn + 1) * 512]
                    st = (sg[:, (ncn % 2) * 512:(ncn % 2 + 1) * 512] if layer == 1
                          else pd[ncn % 2][:])
                    v.tensor_scalar(st, hs, ca[:, 11:12], MAGIC, op0=ALU.mult, op1=ALU.add)
                    v.tensor_scalar(qs, st, MAGIC, None, op0=ALU.subtract)
                    b.inc(v.memset(ca[:, 15:16], 0.0), "vqh" + L)

                wqh = b.n("vqh" + L)
                wqtf = b.n("ptf" + L)
                wtd_prev = b.n("td" + L)

                @blk.tensor
                def _(tn, ncn=ncn, wqh=wqh, wqtf=wqtf, qs=qs, ptb=ptb):
                    tn.wait_ge(b.S("vqh" + L), wqh)
                    wq2 = wqtf - 1 if layer == 2 else wqtf
                    if wq2 > 0:
                        tn.wait_ge(b.S("ptf" + L), wq2)
                    ins = None
                    for i in range(4):
                        ins = tn.transpose(ptb[:, i * 128:(i + 1) * 128],
                                           qs[:, i * 128:(i + 1) * 128], identb[:])
                    b.inc(ins, "tqh" + L)

                wtq = b.n("tqh" + L)

                @blk.scalar
                def _(s, ncn=ncn, wtq=wtq, wtd_prev=wtd_prev, ptb=ptb):
                    s.wait_ge(b.S("tqh" + L), wtq)
                    if wtd_prev > 0 and ncn == 0:
                        s.wait_ge(b.S("td" + L), wtd_prev)
                    s.copy(qhT[:, ncn * 512:(ncn + 1) * 512], ptb[:, 0:512])
                    b.inc(s.copy(colsA[:, 30:31], colsA[:, 30:31]),
                          "ptf" + L)

            wqt = b.n("ptf" + L)

            for n2 in range(2):
                wfd = b.n("vr" + L)

                @blk.tensor
                def _(tn, n2=n2, wfd=wfd):
                    if n2 == 0:
                        tn.wait_ge(b.S("ptf" + L), wqt)
                    if wfd >= 2:
                        tn.wait_ge(b.S("vr" + L), wfd - 1)
                    ins = None
                    for k in range(32):
                        ins = tn.matmul(pd[n2][:], qhT[:, k * 128:(k + 1) * 128],
                                        td[:, k * E + n2 * 512: k * E + n2 * 512 + 512],
                                        start=(k == 0), stop=(k == 31))
                    b.inc(ins, "td" + L)

                wtd = b.n("td" + L)
                wof = b.n("out")

                @blk.vector
                def _(v, n2=n2, wtd=wtd, wof=wof):
                    v.wait_ge(b.S("td" + L), wtd)
                    if layer == 1:
                        b.inc(v.scalar_tensor_tensor(rsl[:, n2 * 512:(n2 + 1) * 512],
                                                     pd[n2][:], ca[:, 14:15],
                                                     rsl[:, n2 * 512:(n2 + 1) * 512],
                                                     op0=ALU.mult, op1=ALU.add), "vr" + L)
                    else:
                        outt = f["outt"]
                        if wof >= 16:
                            v.wait_ge(b.S("out"), wof)
                        v.scalar_tensor_tensor(
                            outt[:, n2 * 512:(n2 + 1) * 512],
                            pd[n2][:], ca[:, 14:15], rsl[:, n2 * 512:(n2 + 1) * 512],
                            op0=ALU.mult, op1=ALU.add)
                        b.inc(v.memset(ca[:, 15:16], 0.0), "vr" + L)

            if layer == 2:
                wvr = b.n("vr" + L)

                @blk.gpsimd
                def _(g):
                    g.wait_ge(b.S("vr" + L), wvr)
                    b.inc(g.dma_start(out=d_out[tt * 128:(tt + 1) * 128, :],
                                      in_=f["outt"][:]), "out", 16)

            if layer == 1 and emit_l1_extras:
                y2tt = f["y2tt"]
                wvr = b.n("vr" + L)
                emit_ln(rsl, ca, "M", (b.S("vr" + L), wvr), y1, "vxq" + L)
                wy2f = b.n("ty2")
                wtx3 = b.n("txq" + L)

                @blk.vector
                def _(v):
                    if wy2f > 0:
                        v.wait_ge(b.S("ty2"), wy2f)
                    if wtx3 > 0:
                        v.wait_ge(b.S("txq" + L), wtx3)
                    v.tensor_copy(xq[:], y1[:])
                    b.inc(v.memset(ca[:, 15:16], 0.0), "vy2b")

                wy2 = b.n("vy2b")
                wfr = b.n("ptf" + L)
                wti_prev = b.n("ti")
                wy2d_prev = b.n("y2d")

                @blk.tensor
                def _(tn, wfr=wfr):
                    tn.wait_ge(b.S("vy2b"), wy2)
                    if wfr > 0:
                        tn.wait_ge(b.S("ptf" + L), wfr)
                    ins = None
                    for i in range(8):
                        ins = tn.transpose(pt[:, i * 128:(i + 1) * 128],
                                           xq[:, i * 128:(i + 1) * 128], identb[:])
                    b.inc(ins, "ty2")

                wt2 = b.n("ty2")

                @blk.scalar
                def _(s, wt2=wt2, wti_prev=wti_prev, wy2d_prev=wy2d_prev):
                    s.wait_ge(b.S("ty2"), wt2)
                    if wti_prev > 0:
                        s.wait_ge(b.S("ti"), wti_prev)
                    if wy2d_prev > 0:
                        s.wait_ge(b.S("y2d"), wy2d_prev)
                    s.copy(y2tt[:], pt[:, 0:1024])
                    b.inc(s.copy(colsA[:, 30:31], colsA[:, 30:31]), "ptf" + L)

                wsy = b.n("ptf" + L)
                bb, jc = tt // 16, tt % 16

                @blk.scalar
                def _(s):
                    s.wait_ge(b.S("ptf" + L), wsy)
                    b.inc(s.dma_start(
                        out=d_y2t[bb, :, jc * 128:(jc + 1) * 128]
                        .rearrange("(kt p) n -> p kt n", p=128),
                        in_=y2tt[:].rearrange("p (kt n) -> p kt n", n=128)), "y2d", 16)

                wpi = b.n("vi")

                @blk.tensor
                def _(tn):
                    tn.wait_ge(b.S("ptf" + L), wsy)
                    if wpi > 0:
                        tn.wait_ge(b.S("vi"), wpi)
                    ins = None
                    for k in range(8):
                        for t2 in range(2):
                            ins = tn.matmul(pi[:, 0:16], y2tt[:, k * 128:(k + 1) * 128],
                                            wqlif[:, k * 2 * H + t2 * H: k * 2 * H + (t2 + 1) * H],
                                            start=(k == 0 and t2 == 0), stop=False)
                    b.inc(ins, "ti")

                wtyh = b.n("ty2")

                @blk.vector
                def _(v):
                    v.wait_ge(b.S("ty2"), wtyh)
                    v.tensor_sub(xq[:], y1[:], xq[:])
                    b.inc(v.memset(ca[:, 15:16], 0.0), "vy2b")

                wy2l = b.n("vy2b")
                wtih = b.n("ti")
                wy2dt = b.n("y2d")
                wfr = b.n("ptf" + L)

                @blk.tensor
                def _(tn, wfr=wfr):
                    tn.wait_ge(b.S("vy2b"), wy2l)
                    if wfr > 0:
                        tn.wait_ge(b.S("ptf" + L), wfr)
                    ins = None
                    for i in range(8):
                        ins = tn.transpose(pt[:, i * 128:(i + 1) * 128],
                                           xq[:, i * 128:(i + 1) * 128], identb[:])
                    b.inc(ins, "ty2")

                wt2 = b.n("ty2")

                @blk.scalar
                def _(s, wt2=wt2):
                    s.wait_ge(b.S("ty2"), wt2)
                    s.wait_ge(b.S("ti"), wtih)
                    s.wait_ge(b.S("y2d"), wy2dt)
                    s.copy(y2tt[:], pt[:, 0:1024])
                    b.inc(s.copy(colsA[:, 30:31], colsA[:, 30:31]), "ptf" + L)

                wsyl = b.n("ptf" + L)

                @blk.tensor
                def _(tn):
                    tn.wait_ge(b.S("ptf" + L), wsyl)
                    ins = None
                    for k in range(8):
                        ins = tn.matmul(pi[:, 0:16], y2tt[:, k * 128:(k + 1) * 128],
                                        wqlif[:, k * 2 * H: k * 2 * H + H],
                                        start=False, stop=(k == 7))
                    b.inc(ins, "ti")

                wti = b.n("ti")

                @blk.vector
                def _(v):
                    v.wait_ge(b.S("ti"), wti)
                    ins = v.tensor_copy(itmp[:, 0:16], pi[:, 0:16])
                    b.chain_inc(v, ins, "vi")
                    base = 32 * bb
                    co = (jc % 2) * 128
                    ins = None
                    for c in range(4):
                        ins = v.transpose(scanIc[base:base + 32, co + c * 32: co + (c + 1) * 32],
                                          itmp[c * 32:(c + 1) * 32, 0:32])
                    b.chain(v, ins, "vch")
                    b.chain(v, v.tensor_scalar(scanIc[base:base + 16, co:co + 128],
                                               scanIc[base:base + 16, co:co + 128],
                                               blife[base:base + 16, 0:1], None,
                                               op0=ALU.add), "vch")


        def finish_tile(layer, pp, yfree_sem):
            L = "L%d" % layer
            yv = b.n(yfree_sem) if layer == 1 else b.n("vqh" + L)
            pv[layer][pp] = {"yfree": yv, "txq" + L: b.n("txq" + L),
                             "ty2": b.n("ty2"), "tgu" + L: b.n("tgu" + L)}

        def emit_tile_barrier(fb, flag=True):
            if flag:
                @blk.vector
                def _(v):
                    b.inc(v.engine_nop(), "bar")

                @blk.scalar
                def _(s):
                    b.inc(s.copy(colsA[:, 30:31], colsA[:, 30:31]), "bar")

                @blk.tensor
                def _(tn):
                    b.inc(tn.transpose(fb["pt"][:, 0:128], identb[:], identb[:]), "bar")

        ffn1sc, f1 = open_ffn_scope(1)
        wx = b.n("x")
        gate_waits = [("se8", b.n("se8")), ("vbc", b.n("vbc")), ("vp", winita)]
        pending = []
        for jc in range(16):
            for bb in (0, 1):
                tt = bb * 16 + jc
                emit_ffn_tile(tt, 1, f1, (b.S("x"), wx),
                              gate_extra_waits=(gate_waits if tt == 0 else None),
                              eo=2 * jc + bb)
                emit_tile_barrier(f1, SERIAL_F1)
                for _ in range(2):
                    if pending:
                        emit_scan_part(*pending.pop(0))
            pending += [(jc, p) for p in range(4)]
        for args in pending:
            emit_scan_part(*args)

        wcnt = b.n("vcnt")
        if STAGE < 2:
            @blk.gpsimd
            def _(g):
                g.wait_ge(b.S("vcnt"), wcnt)
                b.inc(g.dma_start(out=d_dbg, in_=counts[:]), "outg", 16)

            for tt in range(NT):
                wvr = b.n("vrL1")
                wof = b.n("out")

                @blk.vector
                def _(v, tt=tt, wvr=wvr, wof=wof):
                    v.wait_ge(b.S("vrL1"), wvr)
                    if wof >= 48:
                        v.wait_ge(b.S("out"), wof - 32)
                    b.inc(v.tensor_copy(f1["hbuf"][:, (tt % 4) * E:((tt % 4) + 1) * E],
                                        resid[:, tt * E:(tt + 1) * E]), "vod")

                wvo = b.n("vod")

                @blk.scalar
                def _(s, tt=tt, wvo=wvo):
                    s.wait_ge(b.S("vod"), wvo)
                    b.inc(s.dma_start(out=d_out[tt * 128:(tt + 1) * 128, :],
                                      in_=f1["hbuf"][:, (tt % 4) * E:((tt % 4) + 1) * E]),
                          "out", 16)

            @blk.scalar
            def _(s, wfin=b.n("out"), wy2fin=b.n("y2d"), wtsp=b.n("tsp"),
                  wdbg=b.n("dbg"), wog=b.n("outg")):
                s.wait_ge(b.S("out"), wfin)
                s.wait_ge(b.S("y2d"), wy2fin)
                s.wait_ge(b.S("tsp"), wtsp)
                if wog:
                    s.wait_ge(b.S("outg"), wog)
                if wdbg:
                    s.wait_ge(b.S("dbg"), wdbg)

        ffn1sc.close()
        ffnw.close()

        resid_ready_attn = [None] * NT
        if STAGE >= 2:
            attn = ExitStack()
            ae = attn.enter_context
            amask_t = ae(nc.sbuf_tensor("amask_t", [128, 16 * H], F32, side="right"))
            ind_tb = ae(nc.sbuf_tensor("ind_tb", [128, 16 * H], BF16, side="right"))
            amask_c = ae(nc.sbuf_tensor("amask_c", [NCK, H], F32, side="right"))
            ind_cb = ae(nc.sbuf_tensor("ind_cb", [NCK, H], BF16, side="right"))
            band01b = ae(nc.sbuf_tensor("band01b", [128, 512], BF16, side="right"))
            dedb = ae(nc.sbuf_tensor("dedb", [NCK, 16 * 128], BF16, side="right"))
            mrow = ae(nc.sbuf_tensor("mrow", [16, SQ], F32, side="right"))
            afl = ae(nc.sbuf_tensor("afl", [128, 1], F32, side="right"))
            mct = ae(nc.sbuf_tensor("mct", [16, NCK], BF16, side="right"))
            y2tb = ae(nc.sbuf_tensor("y2tb", [128, 8 * SQ], BF16, side="right"))
            wq_q = ae(nc.sbuf_tensor("wq_q", [128, 8 * 128], BF16, side="right"))
            wq_k = ae(nc.sbuf_tensor("wq_k", [128, 8 * 128], BF16, side="right"))
            wq_v = ae(nc.sbuf_tensor("wq_v", [128, 8 * 128], BF16, side="right"))
            qT2 = ae(nc.sbuf_tensor("qT2", [128, SQ], BF16, side="right"))
            kT2 = ae(nc.sbuf_tensor("kT2", [128, SQ], BF16, side="right"))
            vT2 = ae(nc.sbuf_tensor("vT2", [128, SQ], BF16, side="right"))
            vnat = ae(nc.sbuf_tensor("vnat", [128, SQ], BF16, side="right"))
            kc = ae(nc.sbuf_tensor("kc", [128, NCK], BF16, side="right"))
            vct = ae(nc.sbuf_tensor("vct", [128, NCK], BF16, side="right"))
            vc = ae(nc.sbuf_tensor("vc", [NCK, 128], BF16, side="right"))
            ebuf = ae(nc.sbuf_tensor("ebuf", [128, 2 * SQ], BF16, side="right"))
            ec = ae(nc.sbuf_tensor("ec", [NCK, SQ], BF16, side="right"))
            aobuf = ae(nc.sbuf_tensor("aobuf", [128, SQ], BF16, side="right"))
            attnT = ae(nc.sbuf_tensor("attnT", [128, 8 * SQ], BF16, side="right"))
            mrow2 = aobuf

            woutb = ae(nc.sbuf_tensor("woutb", [128, 8 * E], BF16, side="right"))
            boutb = ae(nc.sbuf_tensor("boutb", [1, E], BF16, side="right"))
            dmat = ae(nc.sbuf_tensor("dmat", [128, 16], F32, side="right"))
            rmat = ae(nc.sbuf_tensor("rmat", [128, 16], F32, side="right"))
            zcol = ae(nc.sbuf_tensor("zcol", [128, 16], F32, side="right"))
            pqk = [ae(nc.psum_tensor(f"pqk{i}", [128, 512], F32, side="right")) for i in range(2)]
            psc = [ae(nc.psum_tensor(f"psc{i}", [128, 512], F32, side="right")) for i in range(2)]
            pcc = ae(nc.psum_tensor("pcc", [NCK, 512], F32, side="right"))
            pdn = ae(nc.psum_tensor("pdn", [128, 512], F32, side="right"))
            pav = ae(nc.psum_tensor("pav", [128, 512], F32, side="right"))
            ptA = ae(nc.psum_tensor("ptA", [128, 1024], BF16, side="right"))

            BAR = [("ti", b.n("ti")), ("tdL1", b.n("tdL1")), ("ptfL1", b.n("ptfL1")),
                   ("vrL1", b.n("vrL1")), ("y2d", b.n("y2d")), ("vi", b.n("vi")),
                   ("vcnt", wcnt), ("tsp", b.n("tsp")), ("tguL1", b.n("tguL1"))]

            @blk.gpsimd
            def _(g):
                for nm, val in BAR:
                    if val > 0:
                        g.wait_ge(b.S(nm), val)
                b.inc(g.dma_start(out=d_ccin, in_=counts[:]), "ccg", 16)
                g.wait_ge(b.S("ccg"), 16)
                b.inc(g.collective_compute("AllReduce", ALU.add,
                                           replica_groups=[[0, 1, 2, 3, 4, 5, 6, 7]],
                                           ins=[d_ccin.opt()], outs=[d_ccout.opt()]), "ccg", 1)
                g.wait_ge(b.S("ccg"), 17)
                b.inc(g.dma_start(out=band01b[:], in_=d_band), "abl", 16)
                b.inc(g.dma_start(out=dedb[:], in_=d_ded), "abl", 16)
                b.inc(g.dma_start(out=woutb[:].rearrange("p (kt n) -> p kt n", n=E),
                                  in_=d_wout.rearrange("(kt p) n -> p kt n", p=128)), "abl", 16)
                b.inc(g.dma_start(out=boutb[:], in_=d_bout.rearrange("(o e) -> o e", o=1)),
                      "abl", 16)

            @blk.sync
            def _(sync):
                sync.wait_ge(b.S("ccg"), 17)
                b.inc(sync.dma_start(out=mrow[:], in_=d_ccout), "cbh", 16)

            wccb = b.n("cbh")

            @blk.vector
            def _(v):
                v.wait_ge(b.S("cbh"), wccb)
                b.chain(v, v.tensor_scalar(mrow2[0:16, :], mrow[:], 0.0, None,
                                           op0=ALU.is_gt), "vch")
                v.tensor_copy(mct[:, 0:4], mrow2[0:16, 0:4])
                v.tensor_copy(mct[:, 4:35],
                              mrow2[0:16, :].rearrange("h (n s) -> h n s", s=64)[:, 1:32, 0:1])
                b.chain(v, v.memset(mct[:, 35:40], 0.0), "vch")
                b.inc(v.memset(afl[:], 0.0), "vmk")

            wmk = b.n("vmk")
            for jt in range(17):
                wvm = b.n("vmv")

                @blk.tensor
                def _(tn, jt=jt, wvm=wvm):
                    if jt == 0:
                        tn.wait_ge(b.S("vmk"), wmk)
                    if wvm > 0:
                        tn.wait_ge(b.S("vmv"), wvm)
                    if jt < 16:
                        b.inc(tn.transpose(ptA[:, 0:16],
                                           mrow2[0:16, jt * 128:(jt + 1) * 128],
                                           identb[0:16, 0:16]), "tmk")
                    else:
                        b.inc(tn.transpose(ptA[0:NCK, 0:16], mct[:], identb[0:16, 0:16]),
                              "tmk")

                wtm = b.n("tmk")

                @blk.vector
                def _(v, jt=jt, wtm=wtm):
                    v.wait_ge(b.S("tmk"), wtm)
                    if jt < 16:
                        v.tensor_copy(ind_tb[:, jt * H:(jt + 1) * H], ptA[:, 0:16])
                        v.tensor_scalar(amask_t[:, jt * H:(jt + 1) * H], ptA[:, 0:16],
                                        1.0, NEG, op0=ALU.is_lt, op1=ALU.mult)
                    else:
                        v.tensor_copy(ind_cb[0:NCK, :], ptA[0:NCK, 0:16])
                        v.tensor_scalar(amask_c[0:NCK, :], ptA[0:NCK, 0:16],
                                        1.0, NEG, op0=ALU.is_lt, op1=ALU.mult)
                    b.inc(v.memset(afl[:], 0.0), "vmv")

            wvmv = b.n("vmv")
            wabl = b.n("abl")

            for bb in range(2):
                wy2p = b.n("tqk")

                @blk.sync
                def _(sync, bb=bb, wy2p=wy2p):
                    for nm, val in BAR:
                        if val > 0:
                            sync.wait_ge(b.S(nm), val)
                    if wy2p > 0:
                        sync.wait_ge(b.S("tqk"), wy2p)
                    b.inc(sync.dma_start(out=y2tb[:].rearrange("p (kt n) -> p kt n", n=SQ),
                                         in_=d_y2t[bb].rearrange("(kt p) n -> p kt n", p=128)),
                          "ay", 16)

                way = b.n("ay")
                for hp in range(8):
                    wfree = b.n("tqk")

                    @blk.gpsimd
                    def _(g, bb=bb, hp=hp, wfree=wfree):
                        if wfree > 0:
                            g.wait_ge(b.S("tqk"), wfree)
                        for dst, c0 in ((wq_q, hp * 128), (wq_k, E + hp * 128),
                                        (wq_v, 2 * E + hp * 128)):
                            b.inc(g.dma_start(
                                out=dst[:].rearrange("p (kt n) -> p kt n", n=128),
                                in_=d_wqkv[:, c0:c0 + 128]
                                .rearrange("(kt p) n -> p kt n", p=128)), "aw", 16)

                    waw = b.n("aw")
                    for pi3, (dstT, wsl, bcol) in enumerate(
                            ((qT2, wq_q, hp), (kT2, wq_k, 8 + hp), (vT2, wq_v, 16 + hp))):
                        for nt in range(4):
                            par = nt % 2
                            wev = b.n("vqk")

                            @blk.tensor
                            def _(tn, bb=bb, pi3=pi3, nt=nt, par=par, wsl=wsl, wev=wev):
                                if pi3 == 0 and nt == 0:
                                    tn.wait_ge(b.S("ay"), way)
                                    tn.wait_ge(b.S("aw"), waw)
                                if wev >= 2:
                                    tn.wait_ge(b.S("vqk"), wev - 1)
                                ins = None
                                for kt in range(8):
                                    ins = tn.matmul(pqk[par][:],
                                                    wsl[:, kt * 128:(kt + 1) * 128],
                                                    y2tb[:, kt * SQ + nt * 512: kt * SQ + nt * 512 + 512],
                                                    start=(kt == 0), stop=(kt == 7))
                                b.inc(ins, "tqk")

                            wq1 = b.n("tqk")
                            wwar = (b.n("tsc"), b.n("tcc"), b.n("tvt"), b.n("tdn"))

                            @blk.vector
                            def _(v, dstT=dstT, pi3=pi3, nt=nt, par=par, bcol=bcol,
                                  wq1=wq1, wwar=wwar):
                                v.wait_ge(b.S("tqk"), wq1)
                                if pi3 == 0 and nt == 0:
                                    for nm, val in zip(("tsc", "tcc", "tvt", "tdn"), wwar):
                                        if val > 0:
                                            v.wait_ge(b.S(nm), val)
                                v.tensor_scalar(dstT[:, nt * 512:(nt + 1) * 512],
                                                pqk[par][:], bqkvc[:, bcol:bcol + 1],
                                                None, op0=ALU.add)
                                b.inc(v.memset(afl[:], 0.0), "vqk")

                    wvq = b.n("vqk")
                    for g4 in range(4):
                        wsv = b.n("svt")

                        @blk.tensor
                        def _(tn, g4=g4, wsv=wsv):
                            if g4 == 0:
                                tn.wait_ge(b.S("vqk"), wvq)
                            if wsv > 0:
                                tn.wait_ge(b.S("svt"), wsv)
                            ins = None
                            for i in range(4):
                                jt = g4 * 4 + i
                                ins = tn.transpose(ptA[:, i * 128:(i + 1) * 128],
                                                   vT2[:, jt * 128:(jt + 1) * 128], identb[:])
                            b.inc(ins, "tvt")

                        wtv = b.n("tvt")

                        wavp = b.n("tav")

                        @blk.scalar
                        def _(s, g4=g4, wtv=wtv, wavp=wavp):
                            s.wait_ge(b.S("tvt"), wtv)
                            if g4 == 0 and wavp > 0:
                                s.wait_ge(b.S("tav"), wavp)
                            s.copy(vnat[:, g4 * 512:(g4 + 1) * 512], ptA[:, 0:512])
                            b.inc(s.copy(afl[:], afl[:]), "svt")

                    wkcc = (b.n("tcc"), b.n("tvc"))

                    @blk.vector
                    def _(v, wkcc=wkcc):
                        for nm, val in zip(("tcc", "tvc"), wkcc):
                            if val > 0:
                                v.wait_ge(b.S(nm), val)
                        v.tensor_copy(kc[:, 0:4], kT2[:, 0:4])
                        v.tensor_copy(kc[:, 4:35],
                                      kT2[:].rearrange("p (n s) -> p n s", s=64)[:, 1:32, 0:1])
                        v.memset(kc[:, 35:40], 0.0)
                        v.tensor_copy(vct[:, 0:4], vT2[:, 0:4])
                        v.tensor_copy(vct[:, 4:35],
                                      vT2[:].rearrange("p (n s) -> p n s", s=64)[:, 1:32, 0:1])
                        b.inc(v.memset(vct[:, 35:40], 0.0), "vkc")

                    wkc = b.n("vkc")
                    wsv2 = b.n("svt")

                    @blk.tensor
                    def _(tn, wkc=wkc, wsv2=wsv2):
                        tn.wait_ge(b.S("vkc"), wkc)
                        tn.wait_ge(b.S("svt"), wsv2)
                        b.inc(tn.transpose(ptA[0:NCK, 0:128], vct[:, 0:NCK], identb[:]), "tvc")

                    wvc = b.n("tvc")

                    wavp2 = b.n("tav")

                    @blk.scalar
                    def _(s, wvc=wvc, wavp2=wavp2):
                        s.wait_ge(b.S("tvc"), wvc)
                        if wavp2 > 0:
                            s.wait_ge(b.S("tav"), wavp2)
                        s.copy(vc[0:NCK, :], ptA[0:NCK, 0:128])
                        b.inc(s.copy(afl[:], afl[:]), "svt")

                    wtao_p = b.n("tao")
                    for hh in range(2):
                        h = hp * 2 + hh
                        hb = hh * 64
                        for qc in range(4):
                            wse = b.n("sec")

                            @blk.tensor
                            def _(tn, qc=qc, hb=hb, wse=wse):
                                if qc == 0:
                                    tn.wait_ge(b.S("svt"), b.n("svt"))
                                    if b.n("tav") > 0:
                                        tn.wait_ge(b.S("tav"), b.n("tav"))
                                if wse > 0:
                                    tn.wait_ge(b.S("sec"), wse)
                                b.inc(tn.matmul(pcc[0:NCK, :], kc[hb:hb + 64, 0:NCK],
                                                qT2[hb:hb + 64, qc * 512:(qc + 1) * 512],
                                                start=True, stop=True), "tcc")

                            wcc2 = b.n("tcc")

                            @blk.scalar
                            def _(s, qc=qc, h=h, wcc2=wcc2):
                                s.wait_ge(b.S("tcc"), wcc2)
                                if qc == 0:
                                    s.wait_ge(b.S("vmv"), wvmv)
                                b.inc(s.activation(ec[0:NCK, qc * 512:(qc + 1) * 512],
                                                   pcc[0:NCK, :], AF.Exp, scale=0.125,
                                                   bias=amask_c[0:NCK, h:h + 1]), "sec")

                            wse2 = b.n("sec")

                            @blk.vector
                            def _(v, qc=qc, wse2=wse2):
                                v.wait_ge(b.S("sec"), wse2)
                                if qc == 0:
                                    v.wait_ge(b.S("abl"), wabl)
                                v.tensor_mul(ec[0:NCK, qc * 512:(qc + 1) * 512],
                                             ec[0:NCK, qc * 512:(qc + 1) * 512],
                                             dedb[0:NCK, qc * 512:(qc + 1) * 512])
                                b.inc(v.memset(afl[:], 0.0), "vec")

                        for qt in range(16):
                            par = qt % 2
                            jts = []
                            if qt > 0:
                                va = 1 if qt == 1 else 0
                                jts.append((qt - 1, 0, va))
                            vbv = 3 if qt == 0 else 2
                            jts.append((qt, 128, vbv))
                            wex = b.n("sex")

                            @blk.tensor
                            def _(tn, qt=qt, par=par, jts=jts, hb=hb, wex=wex):
                                if wex >= 2:
                                    tn.wait_ge(b.S("sex"), wex - 1)
                                ins = None
                                for (jt, co, va) in jts:
                                    ins = tn.matmul(psc[par][:, co:co + 128],
                                                    kT2[hb:hb + 64, jt * 128:(jt + 1) * 128],
                                                    qT2[hb:hb + 64, qt * 128:(qt + 1) * 128],
                                                    start=True, stop=True)
                                b.inc(ins, "tsc")

                            wts = b.n("tsc")

                            wavh = b.n("tav")

                            @blk.scalar
                            def _(s, qt=qt, par=par, jts=jts, h=h, wts=wts, wavh=wavh):
                                s.wait_ge(b.S("tsc"), wts)
                                if qt == 0:
                                    s.wait_ge(b.S("vmv"), wvmv)
                                if qt == 0 and wavh > 0:
                                    s.wait_ge(b.S("tav"), wavh)
                                ins = None
                                for (jt, co, va) in jts:
                                    ins = s.activation(ebuf[:, qt * 256 + co: qt * 256 + co + 128],
                                                       psc[par][:, co:co + 128], AF.Exp,
                                                       scale=0.125,
                                                       bias=amask_t[:, jt * H + h: jt * H + h + 1])
                                b.inc(ins, "sex")

                            wsx = b.n("sex")

                            @blk.vector
                            def _(v, qt=qt, jts=jts, wsx=wsx):
                                v.wait_ge(b.S("sex"), wsx)
                                if qt == 0:
                                    v.wait_ge(b.S("abl"), wabl)
                                ins = None
                                for (jt, co, va) in jts:
                                    v.tensor_mul(
                                        ebuf[:, qt * 256 + co: qt * 256 + co + 128],
                                        ebuf[:, qt * 256 + co: qt * 256 + co + 128],
                                        band01b[:, va * 128:(va + 1) * 128])
                                b.inc(v.memset(afl[:], 0.0), "vbm")

                        wbm = b.n("vbm")
                        wvec = b.n("vec")
                        wvdm = b.n("vdm")

                        @blk.tensor
                        def _(tn, h=h, wbm=wbm, wvec=wvec, wvdm=wvdm):
                            tn.wait_ge(b.S("vbm"), wbm)
                            tn.wait_ge(b.S("vec"), wvec)
                            if wvdm > 0:
                                tn.wait_ge(b.S("vdm"), wvdm)
                            ins = None
                            for qt in range(16):
                                jts2 = ([(qt - 1, 0)] if qt > 0 else []) + [(qt, 128)]
                                first = True
                                for (jt, co) in jts2:
                                    tn.matmul(pdn[:, qt:qt + 1],
                                              ebuf[:, qt * 256 + co: qt * 256 + co + 128],
                                              ind_tb[:, jt * H + h: jt * H + h + 1],
                                              start=first, stop=False)
                                    first = False
                                ins = tn.matmul(pdn[:, qt:qt + 1],
                                                ec[0:NCK, qt * 128:(qt + 1) * 128],
                                                ind_cb[0:NCK, h:h + 1],
                                                start=False, stop=True)
                            b.inc(ins, "tdn")

                        wdn = b.n("tdn")

                        @blk.vector
                        def _(v, wdn=wdn):
                            v.wait_ge(b.S("tdn"), wdn)
                            b.inc(v.tensor_copy(dmat[:], pdn[:, 0:16]), "vdm")

                        wvd = b.n("vdm")

                        @blk.vector
                        def _(v, wvd=wvd):
                            v.wait_ge(b.S("vdm"), wvd)
                            b.chain(v, v.tensor_scalar(zcol[:], dmat[:], 0.0, None,
                                                       op0=ALU.is_le), "vch")
                            b.chain(v, v.tensor_add(dmat[:], dmat[:], zcol[:]), "vch")
                            b.inc(v.reciprocal(rmat[:], dmat[:]), "vrm")

                        wrm = b.n("vrm")
                        for half in range(2):
                            wao = b.n("vao")

                            @blk.tensor
                            def _(tn, half=half, hb=hb, wao=wao):
                                if half == 0:
                                    tn.wait_ge(b.S("vrm"), wrm)
                                if wao > 0:
                                    tn.wait_ge(b.S("vao"), wao)
                                ins = None
                                for qt in range(half * 8, half * 8 + 8):
                                    jts2 = ([(qt - 1, 0)] if qt > 0 else []) + [(qt, 128)]
                                    pcol = (qt % 8) * 64
                                    first = True
                                    for (jt, co) in jts2:
                                        tn.matmul(pav[:, pcol:pcol + 64],
                                                  ebuf[:, qt * 256 + co: qt * 256 + co + 128],
                                                  vnat[:, jt * 128 + hb: jt * 128 + hb + 64],
                                                  start=first, stop=False)
                                        first = False
                                    ins = tn.matmul(pav[:, pcol:pcol + 64],
                                                    ec[0:NCK, qt * 128:(qt + 1) * 128],
                                                    vc[0:NCK, hb:hb + 64],
                                                    start=False, stop=True)
                                b.inc(ins, "tav")

                            wav = b.n("tav")

                            @blk.vector
                            def _(v, half=half, hh=hh, hb=hb, wav=wav):
                                v.wait_ge(b.S("tav"), wav)
                                if half == 0 and hh == 0 and wtao_p > 0:
                                    v.wait_ge(b.S("tao"), wtao_p)
                                for qt in range(half * 8, half * 8 + 8):
                                    pcol = (qt % 8) * 64
                                    v.tensor_scalar(aobuf[:, qt * 128 + hb: qt * 128 + hb + 64],
                                                    pav[:, pcol:pcol + 64],
                                                    rmat[:, qt:qt + 1], None,
                                                    op0=ALU.mult)
                                b.inc(v.memset(afl[:], 0.0), "vao")

                    wvo = b.n("vao")
                    for g4 in range(4):
                        wsa = b.n("svt")

                        @blk.tensor
                        def _(tn, g4=g4, wsa=wsa, wvo=wvo):
                            if g4 == 0:
                                tn.wait_ge(b.S("vao"), wvo)
                            if wsa > 0:
                                tn.wait_ge(b.S("svt"), wsa)
                            ins = None
                            for i in range(4):
                                qt = g4 * 4 + i
                                ins = tn.transpose(ptA[:, i * 128:(i + 1) * 128],
                                                   aobuf[:, qt * 128:(qt + 1) * 128], identb[:])
                            b.inc(ins, "tao")

                        wta = b.n("tao")

                        wtpo_p = b.n("tpo")

                        @blk.scalar
                        def _(s, bb=bb, hp=hp, g4=g4, wta=wta, wtpo_p=wtpo_p):
                            s.wait_ge(b.S("tao"), wta)
                            if hp == 0 and g4 == 0 and wtpo_p > 0:
                                s.wait_ge(b.S("tpo"), wtpo_p)
                            s.copy(attnT[:, hp * SQ + g4 * 512: hp * SQ + (g4 + 1) * 512],
                                   ptA[:, 0:512])
                            b.inc(s.copy(afl[:], afl[:]), "svt")

                wat = b.n("svt")
                for mt in range(16):
                    tt = bb * 16 + mt
                    for n2 in range(2):
                        wvr2 = b.n("vaor")

                        @blk.tensor
                        def _(tn, bb=bb, mt=mt, n2=n2, wvr2=wvr2):
                            if mt == 0 and n2 == 0:
                                tn.wait_ge(b.S("svt"), wat)
                                tn.wait_ge(b.S("abl"), b.n("abl"))
                            if wvr2 >= 2:
                                tn.wait_ge(b.S("vaor"), wvr2 - 1)
                            for hp in range(8):
                                tn.matmul(pqk[n2][:],
                                          attnT[:, hp * SQ + mt * 128: hp * SQ + mt * 128 + 128],
                                          woutb[:, hp * E + n2 * 512: hp * E + n2 * 512 + 512],
                                          start=(hp == 0), stop=False)
                            b.inc(tn.matmul(pqk[n2][:], ones1b[:],
                                            boutb[0:1, n2 * 512:(n2 + 1) * 512],
                                            start=False, stop=True), "tpo")

                        wpo = b.n("tpo")

                        @blk.vector
                        def _(v, tt=tt, n2=n2, wpo=wpo):
                            v.wait_ge(b.S("tpo"), wpo)
                            rsl = resid[:, tt * E:(tt + 1) * E]
                            b.inc(v.scalar_tensor_tensor(rsl[:, n2 * 512:(n2 + 1) * 512],
                                                         pqk[n2][:], 1.0,
                                                         rsl[:, n2 * 512:(n2 + 1) * 512],
                                                         op0=ALU.mult, op1=ALU.add), "vaor")

                    resid_ready_attn[tt] = ("vaor", b.n("vaor"))

            if STAGE >= 3:
                attn.close()
                ffnw2 = ExitStack()
                tgr = ffnw2.enter_context(nc.sbuf_tensor("tgr", [128, 8 * INNER], FP8))
                tur = ffnw2.enter_context(nc.sbuf_tensor("tur", [128, 8 * INNER], FP8))
                tdr = ffnw2.enter_context(nc.sbuf_tensor("tdr", [128, 32 * E], FP8))

                AFIN = [(nm, b.n(nm)) for nm in
                        ("vaor", "vao", "svt", "sex", "sec", "vdm", "vqk", "vbm",
                         "vec", "tqk", "tsc", "tcc", "tdn", "tav", "tao", "tvt",
                         "tvc", "tmk", "vmv", "vkc", "tpo", "vmk", "vrm")]

                @blk.sync
                def _(sync):
                    for nm, val in AFIN:
                        if val > 0:
                            sync.wait_ge(b.S(nm), val)
                    for dst, slot in ((tgr, 0), (tur, 1), (tdr, 2)):
                        b.inc(sync.dma_start(out=dst[:], in_=d_tern[slot]), "wl2", 16)

                wl2fin = b.n("wl2")

                @blk.tensor
                def _(tn):
                    for nm, val in AFIN:
                        if val > 0:
                            tn.wait_ge(b.S(nm), val)

                tg, tu, td = tgr, tur, tdr
                f2sc, f2 = open_ffn_scope(2)
                rfin = ("vaor", b.n("vaor"))
                for tt in range(NT):
                    emit_ffn_tile(tt, 2, f2, (b.S(rfin[0]), rfin[1]),
                                  gate_extra_waits=([("wl2", wl2fin)] if tt == 0 else None),
                                  emit_l1_extras=False, phase="front")
                    if tt >= 1:
                        emit_ffn_tile(tt - 1, 2, f2, None, phase="back")
                emit_ffn_tile(NT - 1, 2, f2, None, phase="back")

                @blk.scalar
                def _(s, wfin=b.n("out")):
                    s.wait_ge(b.S("out"), wfin)

                f2sc.close()
                ffnw2.close()

            if STAGE == 2:
                @blk.gpsimd
                def _(g):
                    g.wait_ge(b.S("cbh"), wccb)
                    b.inc(g.dma_start(out=d_dbg, in_=counts[:]), "outg", 16)

                for tt in range(NT):
                    rr = resid_ready_attn[tt]

                    @blk.gpsimd
                    def _(g, tt=tt, rr=rr):
                        g.wait_ge(b.S(rr[0]), rr[1])
                        b.inc(g.dma_start(out=d_out[tt * 128:(tt + 1) * 128, :],
                                          in_=resid[:, tt * E:(tt + 1) * E]), "outg", 16)

                wog = b.n("outg")
                wdbg = b.n("dbg")

                @blk.scalar
                def _(s):
                    s.wait_ge(b.S("outg"), wog)
                    if wdbg:
                        s.wait_ge(b.S("dbg"), wdbg)

            attn.close()


    b.top.close()
    return nc


def kernel(**inputs):
    x = np.ascontiguousarray(np.asarray(inputs["L0_attn_x"], np.float32))
    ident, band01, ded, ones = host_constants()
    nc = build_program()
    common = dict(
        d_wqkv=np.ascontiguousarray(np.asarray(inputs["L0_attn_Wqkv"], np.float32)),
        d_bqkv=np.ascontiguousarray(np.asarray(inputs["L0_attn_bqkv"], np.float32)),
        d_wlif=np.ascontiguousarray(np.asarray(inputs["L0_attn_Wlif"], np.float32)),
        d_blif=np.ascontiguousarray(np.asarray(inputs["L0_attn_blif"], np.float32)),
        d_wout=np.ascontiguousarray(np.asarray(inputs["L0_attn_Wout"], np.float32)),
        d_bout=np.ascontiguousarray(np.asarray(inputs["L0_attn_bout"], np.float32)),
        d_wg=np.ascontiguousarray(np.asarray(inputs["L0_ffn_w_gate"], np.float32)),
        d_wu=np.ascontiguousarray(np.asarray(inputs["L0_ffn_w_up"], np.float32)),
        d_wd=np.ascontiguousarray(np.asarray(inputs["L0_ffn_w_down"], np.float32)),
        d_ident=ident, d_band=band01, d_ded=ded, d_ones=ones,
    )
    in_maps = []
    for c in range(8):
        m = dict(common)
        m["d_x"] = np.ascontiguousarray(x[2 * c:2 * c + 2].reshape(2 * SQ, E))
        in_maps.append(m)
    res = run_bass_kernel_spmd(nc, in_maps, core_ids=list(range(8)))
    out = np.empty((16, SQ, E), np.float32)
    for c in range(8):
        out[2 * c:2 * c + 2] = np.asarray(res.results[c]["d_out"]).reshape(2, SQ, E)
    return out
